# version 30
# baseline (speedup 1.0000x reference)
import numpy as np
from contextlib import ExitStack

import concourse.bass as bass
import concourse.mybir as mybir
from concourse.bass_utils import run_bass_kernel_spmd

F32 = mybir.dt.float32
BF16 = mybir.dt.bfloat16
I32 = mybir.dt.int32
AF = mybir.ActivationFunctionType
ALU = mybir.AluOpType
AX = mybir.AxisListType
IOA = bass.IndirectOffsetOnAxis

B, N, D = 8, 4096, 1024
H = 16
NCH = 32
STR = 1028
NSUP = 1024
NG = NSUP // 128
NITER = 49
NBIS = 46
DH, DL = 4096, 512
BIGF = float(1 << 20)
PIH = 1.5707963267948966

ENGINES = ("sync", "scalar", "vector", "tensor", "gpsimd")

DEBUG_SHAPES = {
    "mod": [1, 6 * D], "modF": [128, 48], "p12": [128, 1024],
    "x": [128, NCH * STR], "hbf": [128, 1024], "s": [128, NCH],
    "es": [128, NCH], "T": [128, 1], "mask": [128, NCH], "msup": [128, NCH],
    "slots": [128, NCH], "carry": [4, 512], "u": [128, NCH], "lohi": [1, 4],
}


class Prog:

    def __init__(self):
        self.ops = []
        self.cnt = {}

    def val(self, sem):
        return self.cnt.get(sem, 0)

    def raw(self, engine, fn, waits=(), incs=(), label="", qsem=None):
        assert engine in ENGINES, engine
        waits = [w for w in waits if w is not None]
        for w in waits:
            if isinstance(w, int):
                assert w < len(self.ops), f"wait on future op at {label}"
        dmav = None
        if qsem is not None:
            self.cnt[qsem] = self.cnt.get(qsem, 0) + 16
            dmav = self.cnt[qsem]
        for s, k in incs:
            self.cnt[s] = self.cnt.get(s, 0) + k
        self.ops.append(dict(engine=engine, fn=fn, waits=waits, label=label,
                             qsem=qsem, dmav=dmav, extra_incs=list(incs),
                             marked=False))
        return len(self.ops) - 1

    def emit(self, block, sems):
        ops = self.ops
        for i, o in enumerate(ops):
            for w in o["waits"]:
                if isinstance(w, int):
                    p = ops[w]
                    if p["qsem"] is None and not (
                            p["engine"] == o["engine"] == "tensor"):
                        p["marked"] = True
        ecnt = {}
        esem = {"sync": "sS", "scalar": "sA", "vector": "sV",
                "tensor": "sT", "gpsimd": "sG"}
        for o in ops:
            if o["marked"]:
                e = o["engine"]
                ecnt[e] = ecnt.get(e, 0) + 1
                o["val"] = ecnt[e]

        epos = {}
        for o in ops:
            e = o["engine"]
            o["pos"] = epos.get(e, 0)
            epos[e] = o["pos"] + 1

        def body(engine_name):
            def go(eng):
                last_drain = -1
                for o in ops:
                    if o["engine"] != engine_name:
                        continue
                    if o["label"] == "__drain__":
                        last_drain = o["pos"]
                        o["fn"](eng)
                        continue
                    merged = {}
                    for w in o["waits"]:
                        if isinstance(w, int):
                            p = ops[w]
                            if p["qsem"] is not None:
                                merged[p["qsem"]] = max(merged.get(p["qsem"], 0), p["dmav"])
                            elif p["engine"] == engine_name and engine_name == "tensor":
                                pass
                            elif p["marked"]:
                                s = esem[p["engine"]]
                                merged[s] = max(merged.get(s, 0), p["val"])
                        else:
                            s, v = w
                            merged[s] = max(merged.get(s, 0), v)
                    for s, v in merged.items():
                        eng.wait_ge(sems[s], v)
                    inst = o["fn"](eng)
                    if inst is not None and o["label"]:
                        try:
                            inst.annotate(o["label"])
                        except Exception:
                            pass
                    if o["qsem"] is not None:
                        assert inst is not None
                        inst = inst.then_inc(sems[o["qsem"]], 16)
                    elif o["marked"]:
                        assert inst is not None, f"marked op without instruction: {o['label']}"
                        inst = inst.then_inc(sems[esem[o["engine"]]], 1)
                    for s, k in o["extra_incs"]:
                        inst = inst.then_inc(sems[s], k)
            return go
        block.sync(body("sync"))
        block.scalar(body("scalar"))
        block.vector(body("vector"))
        block.tensor(body("tensor"))
        block.gpsimd(body("gpsimd"))


class Builder:
    def __init__(self, nc, dbg=()):
        self.nc = nc
        self.pr = Prog()
        self.T = {}
        self.dbg = set(dbg)
        self.ctx = ExitStack()
        self.M = {}

    def dram(self, name, shape, dtype=F32, kind="ExternalInput"):
        self.T[name] = self.nc.dram_tensor(name, list(shape), dtype, kind=kind).ap()
        return self.T[name]

    def sb(self, name, shape, dtype=F32):
        h = self.ctx.enter_context(self.nc.sbuf_tensor(name, list(shape), dtype))
        self.T[name] = h
        return h

    def ps(self, name, shape, dtype=F32):
        h = self.ctx.enter_context(self.nc.psum_tensor(name, list(shape), dtype))
        self.T[name] = h
        return h

    def t(self, name):
        v = self.T[name]
        return v if isinstance(v, bass.AP) else v[:]

    QCHAINS = {"sync": ["qS", "qS2", "qS3"], "scalar": ["qS", "qS2", "qS3"],
               "gpsimd": ["qG", "qG2", "qG3"]}

    def dma(self, queue, dst, src, waits=(), label="", chain=None):
        if not hasattr(self, "_lastdma"):
            self._lastdma = {}
            self._rr = {}
        if chain is None:
            ch = self.QCHAINS[queue]
            k = self._rr.get(queue, 0)
            self._rr[queue] = (k + 1) % len(ch)
            q = ch[k]
        else:
            q = chain
        prevq = self._lastdma.get(q)
        m = self.pr.raw(queue, lambda e, d=dst, s=src: e.dma_start(d, s),
                        waits=list(waits) + ([prevq] if prevq is not None else []),
                        label=label, qsem=q)
        self._lastdma[q] = m
        return m

    def op(self, engine, fn, waits=(), label=""):
        return self.pr.raw(engine, fn, waits=waits, label=label)

    def dr(self, engine):
        self.pr.raw(engine, lambda e: e.drain(), label="__drain__")


def build(dbg=()):
    nc = bass.Bass("TRN2", target_bir_lowering=False)
    b = Builder(nc, dbg)
    _declare(b)
    with b.ctx:
        _prep(b)
        _attn(b)
        _tail(b)
        sems = {}
        for s in ("qS", "qS2", "qS3", "qG", "qG2", "qG3",
                  "sA", "sV", "sT", "sG", "sS"):
            sems[s] = b.ctx.enter_context(nc.semaphore(s))
        with nc.Block() as block:
            b.pr.emit(block, sems)
    return nc


def _declare(b):
    d, sb, ps = b.dram, b.sb, b.ps
    d("x_in", [N, D]); d("cvecF", [128, 8]); d("wmod", [D, 6 * D])
    d("WV", [D, D]); d("OW", [D, D]); d("K123", [D, 48])
    d("trigF", [128, 40])
    d("rtok", [1, D]); d("lgF", [128, 8]); d("hgF", [128, 8])
    d("lw1", [D, DL]); d("lw2", [DL, D]); d("hw1", [D, DH]); d("hw2", [DH, D])
    d("lb1", [1, DL]); d("lb2", [1, D]); d("hb1", [1, DH]); d("hb2", [1, D])
    d("c_U", [128, 128]); d("c_Us", [128, 128]); d("c_id", [128, 128])
    d("c_ones1", [1, 128]); d("c_ones2", [128, 128])
    d("c_rs", [4, 3 * 128]); d("c_oh3", [128, 9])
    d("c_rrow", [128, 128]); d("c_ncrow", [128, NCH]); d("c_iotaw", [128, NCH * 8], I32)
    d("y", [N, D], kind="ExternalOutput")
    d("d_mod", [1, 6 * D], kind="Internal")
    d("d_x", [N, STR], kind="Internal")
    d("d_goff", [NSUP, 8], I32, kind="Internal")
    for nm in b.dbg:
        d("dbg_" + nm, DEBUG_SHAPES[nm], kind="ExternalOutput")

    ps("pVA", [128, 512]); ps("pVB", [128, 512]); ps("pX", [128, 512])
    ps("pCA", [128, 512]); ps("pCB", [128, 512]); ps("pCR", [128, 512])
    ps("pT", [128, 1024], BF16); ps("pTOT", [128, 512])

    sb("x", [128, NCH * STR])
    sb("WVb", [128, 8 * 1024], BF16)
    sb("OWb", [128, 8 * 1024], BF16)
    sb("K123b", [128, 8 * 48], BF16)
    sb("Ub", [128, 128], BF16)
    sb("idb", [128, 128], BF16)
    sb("ones1b", [1, 128], BF16); sb("ones1f", [1, 128], F32)
    sb("onescol", [128, 1], F32)
    sb("rsb", [4, 3 * 128], BF16); sb("oh3b", [128, 9], BF16)
    sb("biasrow", [1, 1072], BF16)

    sb("CB1b", [128, 1024]); sb("SBb", [128, 1024]); sb("Rb", [128, 1024])
    sb("carry_acc", [4, 512])
    sb("cvt", [128, 1024], BF16)
    sb("sbr_col", [128, 1])
    sb("p1s", [128, 512]); sb("p2s", [128, 512])
    sb("s_sb", [128, NCH]); sb("es", [128, NCH])
    sb("hbf", [128, 1024], BF16); sb("hT", [128, 1024], BF16)
    sb("crbf", [128, 16], BF16)
    sb("carry", [4, 512], BF16)
    sb("big1", [128, 2048])
    sb("outbf", [128, 1024], BF16); sb("outT", [128, 1024], BF16)
    sb("cp1", [128, 16]); sb("cpc", [128, 16]); sb("dxc", [128, 32]); sb("dx", [128, 32])
    sb("p2c", [128, 16]); sb("p2e", [128, 16]); sb("pe2x", [128, 16])
    sb("den", [128, 16]); sb("rden", [128, 16]); sb("dent", [128, 16])
    sb("bst", [128, 8]); sb("bcnt", [128, 1], BF16)
    sb("h2", [128, 1024])
    sb("tA", [128, 1]); sb("tB", [128, 1]); sb("tC", [128, 1]); sb("tD", [128, 1])
    sb("tE", [128, 1]); sb("tG", [128, 1]); sb("tH", [128, 1])
    sb("r0", [1, 8]); sb("r1", [1, 8]); sb("kcol", [128, 6])

    T = b.T
    x = T["x"]
    base = 17 * STR

    def xv(off, nfloat):
        assert off + nfloat <= (NCH - 17) * STR, (off, nfloat)
        return x[:, base + off: base + off + nfloat]
    T["v_g1"] = xv(0, 1024)
    T["v_g2"] = xv(1024, 1024)
    T["v_sv"] = xv(2048, 256)
    T["v_cvv"] = xv(2304, 256)
    T["v_su"] = xv(2560, 128)
    T["v_cu"] = xv(2688, 128)
    T["v_rr"] = xv(2816, 128)
    T["v_ncr"] = xv(2944, 32)
    T["v_wpcA"] = xv(3072, 1024).bitcast(BF16)
    T["v_wpcB"] = xv(4096, 1024).bitcast(BF16)
    T["v_GAb"] = xv(5120, 1024)
    T["v_row1"] = xv(6144, 1024)[0:1, :]
    T["v_row2"] = xv(7168, 1024)[0:1, :]
    T["v_stg"] = xv(8192, 128)[0:48, :]
    T["modrow"] = xv(8704, 6144)[0:1, :]
    T["modF"] = xv(14848, 48)
    T["modFb"] = xv(14896, 24).bitcast(BF16)
    T["trigp"] = xv(14920, 40)
    T["tF3"] = T["p2s"][:, 96:128]
    T["slot_i32"] = T["p2s"][:, 384:416].bitcast(I32)
    T["eqb"] = T["p2s"][:, 416:432].bitcast(BF16)
    T["supb"] = T["p2s"][:, 432:448].bitcast(BF16)
    T["rowP"] = T["p2s"][:, 448:480][0:1, :]
    T["rowQ"] = T["p2s"][:, 480:512][0:1, :]
    T["rowR"] = T["p1s"][:, 480:512][0:1, :]
    T["hb2gb"] = T["carry_acc"][:, 0:512].bitcast(BF16)[0:1, :]
    T["tF"] = T["p1s"][:, 416:448]
    T["tF2"] = T["p1s"][:, 448:480]
    T["t1v"] = T["big1"][:, 0:1024]
    T["t2v"] = T["big1"][:, 1024:2048]
    T["z2"] = T["big1"][:, 1024:2048]
    T["z2a"] = T["big1"][:, 0:1024]
    T["hb1bA"] = T["Rb"][:, 0:1024].bitcast(BF16)[0:1, :]
    T["hb1bB"] = T["big1"][:, 0:1024].bitcast(BF16)[0:1, :]
    T["iotaw"] = T["p1s"][:, 0:256].bitcast(I32)
    T["msk"] = T["p1s"][:, 256:288]
    T["gehi"] = T["p1s"][:, 288:320]
    T["eqm"] = T["p1s"][:, 320:352]
    T["u_sb"] = T["p1s"][:, 352:384]
    T["msup"] = T["p1s"][:, 384:416]
    T["goff"] = T["p2s"][:, 0:NG * 8].bitcast(I32)
    T["lb1b"] = T["p2s"][:, 128:384].bitcast(BF16)[0:1, :]
    T["cvA"] = T["cvt"][:, 0:512]
    T["cvB"] = T["cvt"][:, 512:1024]
    T["GBb2"] = T["Rb"][:]
    T["lb2gb"] = T["biasrow"][0:1, 0:1024]
    T["v_lw1b"] = T["WVb"][:, 0:4096]
    T["v_lw2b"] = T["WVb"][:, 4096:8192]
    T["v_xg2"] = T["WVb"][:, 0:2056].bitcast(F32)
    T["hT2"] = T["WVb"][:, 4096:5120]
    T["v_hw1b"] = x[:, 0:16384].bitcast(BF16)
    T["v_hw2b"] = x[:, 16448:32832].bitcast(BF16)
    T["v_xg"] = T["OWb"][:, 0:2056].bitcast(F32)
    T["v_hfA"] = T["OWb"][:, 2056:4104].bitcast(F32)
    T["v_hfB"] = T["OWb"][:, 4104:6152].bitcast(F32)
    T["v_gob"] = T["OWb"][:, 6152:7176]
    T["onesq"] = T["OWb"][:, 7176:7432].bitcast(F32)
    T["onesqb"] = T["OWb"][:, 7432:7560]


def _prep(b):
    t, op, dma, dr, M = b.t, b.op, b.dma, b.dr, b.M
    pr = b.pr
    x = t("x")

    for i, v in enumerate((PIH, 1.0 / D, 1e-5, -0.5, 1e-24, 3.4657359027997265)):
        op("vector", lambda e, i=i, v=v: e.memset(t("kcol")[:, i:i + 1], v), label=f"kc{i}")
    M["kcol"] = op("vector", lambda e: e.engine_nop(), label="kcol")
    M["idb"] = dma("gpsimd", t("idb")[:], t("c_id"), label="ld_idb")
    M["Ub"] = dma("gpsimd", t("Ub")[:], t("c_U"), label="ld_U")
    M["o1b"] = dma("gpsimd", t("ones1b")[:], t("c_ones1"), label="ld_o1b")
    M["rsb"] = dma("gpsimd", t("rsb")[:], t("c_rs"), label="ld_rs")
    M["oh3b"] = dma("gpsimd", t("oh3b")[:], t("c_oh3"), label="ld_oh3")
    M["o1f"] = dma("sync", t("ones1f")[:], t("c_ones1"), label="ld_o1f")
    M["onescol"] = dma("sync", t("onescol")[:], t("c_ones1").rearrange("o p -> p o"), label="ld_onescol")
    M["trig"] = dma("sync", t("trigp")[:], t("trigF"), label="ld_trig")
    M["rr"] = dma("sync", t("v_rr"), t("c_rrow"), label="ld_rrow")
    M["ncr"] = dma("sync", t("v_ncr"), t("c_ncrow"), label="ld_ncrow")

    m_cv = dma("sync", t("h2")[:, 0:8], t("cvecF"), label="ld_cvec")
    m_silu = op("scalar", lambda e: e.activation(t("hbf")[:, 0:8], t("h2")[:, 0:8], AF.Silu),
                waits=[m_cv], label="silu_c")

    b.mx = [None] * NCH
    for c in range(17):
        b.mx[c] = dma("sync", x[:, c * STR: c * STR + 1024],
                      t("x_in")[c * 128:(c + 1) * 128, :], label=f"ld_x{c}")
    for c in range(17):
        op("vector", lambda e, c=c: e.memset(x[:, c * STR + 1024: (c + 1) * STR], 0.0),
           waits=[b.mx[c]], label=f"padz{c}")

    qb = ["pVA", "pVB", "pX", "pCA"]
    wp = {0: t("v_wpcA"), 1: t("v_wpcB")}
    prev_use = {0: None, 1: None}
    m_quad_ev = {}
    mm = None
    for q in range(3):
        for s8 in range(8):
            buf = (q * 8 + s8) % 2
            w = [] if prev_use[buf] is None else [prev_use[buf]]
            mld = dma("gpsimd", wp[buf],
                      t("wmod")[s8 * 128:(s8 + 1) * 128, q * 2048:(q + 1) * 2048],
                      waits=w, label=f"ld_wm{q}_{s8}")
            for g in range(4):
                wts = [mld, m_silu] + ([mm] if mm is not None else [])
                if s8 == 0 and g == 0 and q > 0:
                    wts.append(m_quad_ev[q - 1])
                mm = op("tensor", lambda e, s8=s8, g=g, buf=buf: e.matmul(
                    t(qb[g])[0:1, :], t("hbf")[:, s8:s8 + 1],
                    wp[buf][:, g * 512:(g + 1) * 512],
                    start=(s8 == 0), stop=(s8 == 7)),
                    waits=wts, label=f"modmm{q}_{s8}_{g}")
            prev_use[buf] = mm
        ev = None
        for g in range(4):
            ev = op("scalar", lambda e, q=q, g=g: e.copy(
                t("modrow")[:, q * 2048 + g * 512: q * 2048 + (g + 1) * 512],
                t(qb[g])[0:1, :]), waits=[mm], label=f"modev{q}_{g}")
        m_quad_ev[q] = ev
    mq0, mq1, mq2 = m_quad_ev[0], m_quad_ev[1], m_quad_ev[2]
    M["modmm_done"] = mq2

    row1, row2 = t("v_row1"), t("v_row2")
    mr = t("modrow")
    m_ca1 = op("vector", lambda e: e.tensor_scalar(mr[:, 1024:2048], mr[:, 1024:2048],
                                                   1.0, None, ALU.add), waits=[mq0], label="ca1")
    m_gs = op("scalar", lambda e: e.sign(row1, mr[:, 2048:3072]), waits=[mq1], label="ga_s")
    m_ga = op("scalar", lambda e: e.activation(row2, mr[:, 2048:3072], AF.Abs),
              waits=[mq1, m_gs], label="ga_a")
    m_gm = op("vector", lambda e: e.tensor_scalar(row2, row2, 0.25, None, ALU.max),
              waits=[m_ga], label="ga_m")
    m_gac = op("vector", lambda e: e.tensor_mul(mr[:, 2048:3072], row1, row2),
               waits=[m_gs, m_gm], label="ga_c")
    dr("vector")
    m_cb1 = op("vector", lambda e: e.tensor_scalar(mr[:, 4096:5120], mr[:, 4096:5120],
                                                   1.0, None, ALU.add), waits=[mq2], label="cb1")
    m_gs2 = op("scalar", lambda e: e.sign(row1, mr[:, 5120:6144]),
               waits=[mq2, m_gac], label="gb_s")
    m_ga2 = op("scalar", lambda e: e.activation(row2, mr[:, 5120:6144], AF.Abs),
               waits=[mq2, m_gac, m_gs2], label="gb_a")
    m_gm2 = op("vector", lambda e: e.tensor_scalar(row2, row2, 0.25, None, ALU.max),
               waits=[m_ga2], label="gb_m")
    m_gbc = op("vector", lambda e: e.tensor_mul(mr[:, 5120:6144], row1, row2),
               waits=[m_gs2, m_gm2], label="gb_c")
    m_modrow = op("vector", lambda e: e.engine_nop(),
                  waits=[m_ca1, m_gac, m_cb1, m_gbc], label="modrow_done")
    if "mod" in b.dbg:
        dma("sync", t("dbg_mod"), mr[:], waits=[m_modrow], label="dbg_mod")

    m_dw = dma("sync", t("d_mod"), mr[:], waits=[m_modrow], label="mod2dram")
    m_drd = dma("sync", t("v_stg"), t("d_mod").rearrange("o (s p) -> (o s) p", p=128),
                waits=[m_dw], label="mod4dram")
    m_dc = op("scalar", lambda e: e.activation(t("hT")[0:48, 0:128], t("v_stg"), AF.Copy),
              waits=[m_drd], label="modF_c")
    m_dt = op("tensor", lambda e: e.transpose(t("pT")[:, 0:48], t("hT")[0:48, 0:128],
              t("idb")[0:48, 0:48]), waits=[m_dc, M["idb"], mq2], label="modF_T")
    m_modF = op("scalar", lambda e: e.copy(t("modF")[:], t("pT")[:, 0:48]),
                waits=[m_dt], label="modF_ev")
    dr("scalar")
    m_modFb = op("scalar", lambda e: e.copy(t("modFb")[:], t("modF")[:]),
                 waits=[m_modF], label="modFb")
    M["modF"] = m_modF
    if "modF" in b.dbg:
        dma("sync", t("dbg_modF"), t("modF")[:], waits=[m_modF], label="dbg_modF")

    m_rt = dma("sync", row1, t("rtok"), waits=[m_gbc], label="ld_rtok")
    last_b = mq2

    def bcast128(dstname, src_row, extraw, label):
        nonlocal last_b
        for hcol in range(2):
            mmx = op("tensor", lambda e, hcol=hcol, src_row=src_row: e.matmul(
                t("pVB")[:, 0:512], t("ones1f"),
                src_row[:, hcol * 512:(hcol + 1) * 512], start=True, stop=True),
                waits=[last_b, M["o1f"]] + extraw, label=f"{label}_mm{hcol}")
            last_b = op("scalar", lambda e, hcol=hcol, dstname=dstname: e.copy(
                t(dstname)[:, hcol * 512:(hcol + 1) * 512], t("pVB")[:, 0:512]),
                waits=[mmx], label=f"{label}_ev{hcol}")
        return last_b

    M["CB1b"] = bcast128("CB1b", mr[:, 4096:5120], [m_cb1], "CB1b")
    M["SBb"] = bcast128("SBb", mr[:, 3072:4096], [mq1], "SBb")
    M["Rb"] = bcast128("Rb", row1, [m_rt], "Rb")
    M["GAb"] = bcast128("v_GAb", mr[:, 2048:3072], [m_gac], "GAb")

    m_sm = op("vector", lambda e: e.tensor_mul(row2, mr[:, 3072:4096], row1),
              waits=[m_rt, mq1, m_gm2, M["Rb"]], label="sbr_mul")
    dr("vector")
    m_sr = op("vector", lambda e: e.reduce_sum(t("r0")[:, 0:1], row2, axis=AX.X),
              waits=[m_sm], label="sbr_red")
    m_sb1 = op("tensor", lambda e: e.matmul(t("pVB")[:, 0:1], t("ones1f"), t("r0")[:, 0:1],
                                            start=True, stop=True),
               waits=[m_sr, last_b], label="sbr_mm")
    M["sbr"] = op("scalar", lambda e: e.copy(t("sbr_col")[:], t("pVB")[:, 0:1]),
                  waits=[m_sb1], label="sbr_ev")

    M["WVb"] = dma("gpsimd", t("WVb")[:].rearrange("p (s o) -> p s o", s=8),
                   t("WV").rearrange("(s p) o -> p s o", p=128), label="ld_WV")
    M["K123b"] = dma("gpsimd", t("K123b")[:].rearrange("p (s o) -> p s o", s=8),
                     t("K123").rearrange("(s p) o -> p s o", p=128), label="ld_K123")
    M["OWb"] = dma("gpsimd", t("OWb")[:].rearrange("p (s o) -> p s o", s=8),
                   t("OW").rearrange("(s p) o -> p s o", p=128), label="ld_OW")
    mm = None
    for grp, (lo, hi, wt) in enumerate(((0, 512, "WVb"), (512, 1024, "WVb"), (0, 48, "K123b"))):
        for s8 in range(8):
            base_ = s8 * (1024 if wt == "WVb" else 48)
            wts = [m_modFb, M[wt]] + ([mm] if mm is not None else [])
            mm = op("tensor", lambda e, s8=s8, lo=lo, hi=hi, wt=wt, base_=base_: e.matmul(
                t("pCB")[0:1, 0:hi - lo], t("modFb")[:, s8:s8 + 1],
                t(wt)[:, base_ + lo: base_ + hi],
                start=(s8 == 0), stop=(s8 == 7)),
                waits=wts, label=f"bias_mm{grp}_{s8}")
        dst = ((0, 512), (512, 1024), (1024, 1072))[grp]
        mm = op("scalar", lambda e, dst=dst: e.copy(
            t("biasrow")[:, dst[0]:dst[1]], t("pCB")[0:1, 0:dst[1] - dst[0]]),
            waits=[mm], label=f"bias_ev{grp}")
    m_bias = mm

    m_wvf = m_bias
    for s in range(8):
        m_wvf = op("vector", lambda e, s=s: e.tensor_scalar(
            t("WVb")[:, s * 1024:(s + 1) * 1024], t("WVb")[:, s * 1024:(s + 1) * 1024],
            t("modF")[:, 8 + s: 9 + s], None, ALU.mult),
            waits=[M["WVb"], m_modF, m_bias, m_wvf], label=f"wvf{s}")
    for s in range(8):
        m_wvf = op("vector", lambda e, s=s: e.tensor_scalar(
            t("K123b")[:, s * 48:(s + 1) * 48], t("K123b")[:, s * 48:(s + 1) * 48],
            t("modF")[:, 8 + s: 9 + s], None, ALU.mult),
            waits=[M["K123b"], m_modF, m_bias, m_wvf], label=f"k123f{s}")
    M["wfold"] = m_wvf

    m_owf = m_wvf
    for s in range(8):
        m_owf = op("vector", lambda e, s=s: e.tensor_tensor(
            t("OWb")[:, s * 1024:(s + 1) * 1024], t("OWb")[:, s * 1024:(s + 1) * 1024],
            t("v_GAb"), ALU.mult),
            waits=[M["OWb"], M["GAb"], m_owf], label=f"owf{s}")
    M["owfold"] = m_owf
    if "wf" in b.dbg:
        dma("sync", t("dbg_wf"), t("WVb")[:], waits=[m_wvf], label="dbg_wf")


    tg = t("trigp")
    m_tb = op("vector", lambda e: e.tensor_scalar(t("dx")[:, 0:8], tg[:, 8:16],
                                                  PIH, None, ALU.add),
              waits=[M["trig"]], label="b1pi")
    m_tb2 = op("vector", lambda e: e.tensor_scalar(t("dxc")[:, 0:8], tg[:, 24:32],
                                                   PIH, None, ALU.add),
               waits=[M["trig"], m_tb], label="b2pi")
    prev_g = {}
    prev_ev = mq2
    for ser in range(2):
        acol = tg[:, 16 * ser: 16 * ser + 8]
        bcol = tg[:, 16 * ser + 8: 16 * ser + 16]
        bpi = t("dx") if ser == 0 else t("dxc")
        pdst = "p1s" if ser == 0 else "p2s"
        for k in range(8):
            w0 = [M["rr"], M["ncr"], M["trig"], m_tb, m_tb2]
            wsu = list(w0)
            if (ser, k) != (0, 0):
                wsu.append(prev_g["g1"])
            msu = op("scalar", lambda e, acol=acol, k=k: e.activation(
                t("v_su"), t("v_rr"), AF.Sin, scale=acol[:, k:k + 1]),
                waits=wsu, label=f"sinu{ser}_{k}")
            wcu = [msu]
            if (ser, k) != (0, 0):
                wcu.append(prev_g["g2"])
            mcu = op("scalar", lambda e, acol=acol, k=k: e.activation(
                t("v_cu"), t("v_rr"), AF.Sin, bias=t("kcol")[:, 0:1], scale=acol[:, k:k + 1]),
                waits=wcu + [M["kcol"]], label=f"cosu{ser}_{k}")
            wg1 = [msu]
            if ser == 1:
                wg1.append(prev_ev)
            mg1 = op("vector", lambda e, k=k: e.tensor_scalar(
                t("v_g1")[:, k * 128:(k + 1) * 128], t("v_su"), tg[:, 32 + k:33 + k],
                None, ALU.mult), waits=wg1, label=f"g1_{ser}_{k}")
            mg2 = op("vector", lambda e, k=k: e.tensor_scalar(
                t("v_g2")[:, k * 128:(k + 1) * 128], t("v_cu"), tg[:, 32 + k:33 + k],
                None, ALU.mult), waits=[mcu, mg1] + ([prev_ev] if ser == 1 else []),
                label=f"g2_{ser}_{k}")
            msv = op("scalar", lambda e, k=k, acol=acol, bcol=bcol: e.activation(
                t("v_sv")[:, k * 32:(k + 1) * 32], t("v_ncr"), AF.Sin,
                bias=bcol[:, k:k + 1], scale=acol[:, k:k + 1]),
                waits=w0 + [mcu] + ([prev_ev] if ser == 1 else []), label=f"sinv{ser}_{k}")
            mcv2 = op("scalar", lambda e, k=k, acol=acol, bpi=bpi: e.activation(
                t("v_cvv")[:, k * 32:(k + 1) * 32], t("v_ncr"), AF.Sin,
                bias=bpi[:, k:k + 1], scale=acol[:, k:k + 1]),
                waits=[msv], label=f"cosv{ser}_{k}")
            for hh in range(2):
                h = 2 * k + hh
                bs = 64 * hh
                mma = op("tensor", lambda e, k=k, bs=bs: e.matmul(
                    t("pX")[:, 0:32],
                    t("v_g1")[bs:bs + 64, k * 128:(k + 1) * 128],
                    t("v_cvv")[bs:bs + 64, k * 32:(k + 1) * 32],
                    start=True, stop=False),
                    waits=[mg2, mcv2, prev_ev, mq2], label=f"pmma{ser}_{h}")
                mmb = op("tensor", lambda e, k=k, bs=bs: e.matmul(
                    t("pX")[:, 0:32],
                    t("v_g2")[bs:bs + 64, k * 128:(k + 1) * 128],
                    t("v_sv")[bs:bs + 64, k * 32:(k + 1) * 32],
                    start=False, stop=True),
                    waits=[mma], label=f"pmmb{ser}_{h}")
                prev_ev = op("scalar", lambda e, h=h, pdst=pdst: e.copy(
                    t(pdst)[:, h * 32:(h + 1) * 32], t("pX")[:, 0:32]),
                    waits=[mmb], label=f"pev{ser}_{h}")
            prev_g = {"g1": mg1, "g2": mg2}
    M["p1"] = M["p2"] = prev_ev
    if "p12" in b.dbg:
        dma("sync", t("dbg_p12")[:, 0:512], t("p1s")[:], waits=[prev_ev], label="dbg_p1")
        dma("sync", t("dbg_p12")[:, 512:1024], t("p2s")[:], waits=[prev_ev], label="dbg_p2")

    M["arena_free"] = b.pr.raw(
        "vector", lambda e: e.engine_nop(),
        waits=[prev_ev, M["owfold"], M["sbr"], m_modFb, M["wfold"]],
        label="arena_free")
    for c in range(17, NCH):
        b.mx[c] = dma("sync", x[:, c * STR: c * STR + 1024],
                      t("x_in")[c * 128:(c + 1) * 128, :],
                      waits=[M["arena_free"]], label=f"ld_x{c}")
        op("vector", lambda e, c=c: e.memset(x[:, c * STR + 1024: (c + 1) * STR], 0.0),
           waits=[b.mx[c]], label=f"padz{c}")
    M["padz_all"] = op("vector", lambda e: e.engine_nop(), label="padz_all")
    def mkreg(e):
        b.bcreg = e.alloc_register("bc")
        return e.reg_mov(b.bcreg, NSUP - 1)
    b.pr.raw("gpsimd", mkreg, label="bcreg")
    M["carry0"] = op("vector", lambda e: e.memset(t("carry")[:], 0.0), label="carry0")
    M["tot0"] = op("vector", lambda e: e.memset(t("carry_acc")[:], 0.0), label="tot0")


def _attn(b):
    t, op, dma, dr, M = b.t, b.op, b.dma, b.dr, b.M
    x = t("x")
    S = [dict() for _ in range(NCH)]
    P0 = {}

    def pv_get(c, key):
        return S[c - 1].get(key) if c else None

    def stage_A(c):
        s = S[c]
        xc = x[:, c * STR: c * STR + 1024]
        s["mean"] = op("vector", lambda e, xc=xc: e.reduce_sum(t("tA"), xc, axis=AX.X),
                       waits=[b.mx[c], pv_get(c, "negm")], label=f"mean{c}")
        s["negm"] = op("vector", lambda e: e.tensor_scalar(t("tB"), t("tA"), -1.0 / D,
                                                           None, ALU.mult),
                       waits=[s["mean"]], label=f"negm{c}")
        sqw = [s["negm"], pv_get(c, "hbf_free")]
        if c == 0:
            sqw.append(M["modF"])
        s["sq"] = op("scalar", lambda e, xc=xc: e.activation(
            t("hbf"), xc, AF.Square, bias=t("tB")[:, 0:1], accum_out=t("tC")[:, 0:1]),
            waits=sqw, label=f"sq{c}")
        s["ln1"] = op("scalar", lambda e: e.activation(t("tC"), t("tC"), AF.Ln,
                      scale=t("kcol")[:, 1:2], bias=t("kcol")[:, 2:3]),
                      waits=[s["sq"], M["kcol"]], label=f"lnv{c}")
        s["rstd"] = op("scalar", lambda e: e.activation(t("tC"), t("tC"), AF.Exp,
                       scale=t("kcol")[:, 3:4]), waits=[s["ln1"]], label=f"rstd{c}")
        hbw = [s["rstd"], s["negm"], pv_get(c, "hbf_free"),
               S[c - 2].get("sred") if c >= 2 else None]
        if c == 0:
            hbw.append(M["modmm_done"])
        s["hbf"] = op("vector", lambda e, xc=xc: e.tensor_scalar(
            t("hbf"), xc, t("tB")[:, 0:1], t("tC")[:, 0:1], ALU.add, ALU.mult),
            waits=hbw, label=f"hbf{c}")

    def stage_B(c):
        s = S[c]
        pT_prev = S[c - 2].get("oTe") if c >= 2 else (S[c - 1].get("hTe") if c else None)
        tw = [s["hbf"], s["sq"], pT_prev]
        if c == 0:
            tw.append(M["idb"])
        m_tp = None
        for j in range(8):
            m_tp = op("tensor", lambda e, j=j: e.transpose(
                t("pT")[:, j * 128:(j + 1) * 128], t("hbf")[:, j * 128:(j + 1) * 128],
                t("idb")[:]), waits=(tw if j == 0 else [m_tp]), label=f"hTt{c}_{j}")
        s["hbf_free"] = m_tp
        s["hTe"] = op("scalar", lambda e: e.copy(t("hT"), t("pT")[:]),
                      waits=[m_tp, pv_get(c, "vals")], label=f"hTe{c}")

    def stage_C(c):
        s = S[c]
        vw = [s["hTe"], M["wfold"], pv_get(c, "t1a")]
        mm = None
        for j in range(8):
            wj = vw if j == 0 else [mm]
            mm = op("tensor", lambda e, j=j: e.matmul(
                t("pCA")[:, 0:48], t("hT")[:, j * 128:(j + 1) * 128],
                t("K123b")[:, j * 48:(j + 1) * 48], start=(j == 0), stop=False),
                waits=wj, label=f"xp{c}_{j}")
        mm = op("tensor", lambda e: e.matmul(t("pCA")[:, 0:48], t("ones1b"),
                                             t("biasrow")[:, 1024:1072],
                                             start=False, stop=True),
                waits=[mm, M["o1b"]], label=f"bk{c}")
        s["xproj"] = mm
        for j in range(8):
            wj = [mm, pv_get(c, "t2b")] if j == 0 else [mm]
            mm = op("tensor", lambda e, j=j: e.matmul(
                t("pVA"), t("hT")[:, j * 128:(j + 1) * 128],
                t("WVb")[:, j * 1024: j * 1024 + 512], start=(j == 0), stop=False),
                waits=wj, label=f"vA{c}_{j}")
            mm = op("tensor", lambda e, j=j: e.matmul(
                t("pVB"), t("hT")[:, j * 128:(j + 1) * 128],
                t("WVb")[:, j * 1024 + 512: (j + 1) * 1024], start=(j == 0), stop=False),
                waits=[mm], label=f"vB{c}_{j}")
        mm = op("tensor", lambda e: e.matmul(t("pVA"), t("ones1b"), t("biasrow")[:, 0:512],
                                             start=False, stop=True), waits=[mm], label=f"bvA{c}")
        mm = op("tensor", lambda e: e.matmul(t("pVB"), t("ones1b"), t("biasrow")[:, 512:1024],
                                             start=False, stop=True), waits=[mm], label=f"bvB{c}")
        s["vals"] = mm

    def stage_D(c):
        s = S[c]
        s["cp1"] = op("vector", lambda e, c=c: e.tensor_tensor(
            t("cp1"), t("pCA")[:, 0:16], t("p1s")[:, c::NCH], ALU.add),
            waits=[s["xproj"], M["p1"], pv_get(c, "cpc")], label=f"cp1{c}")
        s["cpc"] = op("vector", lambda e: e.tensor_scalar(t("cpc"), t("cp1"), -20.0, 20.0,
                                                          ALU.max, ALU.min),
                      waits=[s["cp1"], pv_get(c, "cr")], label=f"cpc{c}")
        s["dxc"] = op("vector", lambda e: e.tensor_scalar(t("dxc"), t("pCA")[:, 16:48],
                                                          -20.0, 20.0, ALU.max, ALU.min),
                      waits=[s["cp1"], pv_get(c, "dx")], label=f"dxc{c}")
        s["cr"] = op("scalar", lambda e: e.activation(t("crbf"), t("cpc"), AF.Exp),
                     waits=[s["cpc"], pv_get(c, "tot")], label=f"cr{c}")
        s["dx"] = op("scalar", lambda e: e.activation(t("dx"), t("dxc"), AF.Exp),
                     waits=[s["dxc"], pv_get(c, "dn2")], label=f"dx{c}")
        s["p2c"] = op("vector", lambda e, c=c: e.tensor_scalar(
            t("p2c"), t("p2s")[:, c::NCH], -20.0, 20.0, ALU.max, ALU.min),
            waits=[M["p2"], pv_get(c, "p2e")], label=f"p2c{c}")
        s["p2e"] = op("scalar", lambda e: e.activation(t("p2e"), t("p2c"), AF.Exp),
                      waits=[s["p2c"], pv_get(c, "pe2x")], label=f"p2e{c}")
        s["pe2x"] = op("vector", lambda e: e.tensor_mul(t("pe2x"), t("p2e"), t("dx")[:, 16:32]),
                       waits=[s["p2e"], s["dx"], pv_get(c, "dn1")], label=f"pe2x{c}")
        cvw = [s["cr"], s["vals"], pv_get(c, "tot")]
        s["cvA"] = op("vector", lambda e: e.tensor_tensor(
            t("cvA").rearrange("p (h a) -> p h a", h=8),
            t("pVA")[:].rearrange("p (h a) -> p h a", h=8),
            t("crbf")[:, 0:8].unsqueeze(2).to_broadcast([128, 8, 64]),
            ALU.mult), waits=cvw, label=f"cvA{c}")
        s["cvB"] = op("vector", lambda e: e.tensor_tensor(
            t("cvB").rearrange("p (h a) -> p h a", h=8),
            t("pVB")[:].rearrange("p (h a) -> p h a", h=8),
            t("crbf")[:, 8:16].unsqueeze(2).to_broadcast([128, 8, 64]),
            ALU.mult), waits=[s["cvA"]], label=f"cvB{c}")
        s["t2a"] = op("vector", lambda e: e.tensor_tensor(
            t("t2v")[:, 0:512].rearrange("p (h a) -> p h a", h=8),
            t("pVA")[:].rearrange("p (h a) -> p h a", h=8),
            t("dx")[:, 0:8].unsqueeze(2).to_broadcast([128, 8, 64]),
            ALU.mult), waits=[s["dx"], s["vals"], s["cvB"], pv_get(c, "nm")],
            label=f"t2a{c}")
        s["t2b"] = op("vector", lambda e: e.tensor_tensor(
            t("t2v")[:, 512:1024].rearrange("p (h a) -> p h a", h=8),
            t("pVB")[:].rearrange("p (h a) -> p h a", h=8),
            t("dx")[:, 8:16].unsqueeze(2).to_broadcast([128, 8, 64]),
            ALU.mult), waits=[s["t2a"]], label=f"t2b{c}")

    def stage_E(c):
        s = S[c]
        cw = [s["cvA"], s["cvB"], pv_get(c, "carry") or M["carry0"], M["Ub"], M["rsb"],
              pv_get(c, "t1a"), s["cp1"], s["dxc"]]
        mcu = op("tensor", lambda e: e.matmul(t("pCA"), t("Ub"), t("cvA"),
                                              start=True, stop=False), waits=cw, label=f"cumA{c}")
        mcu = op("tensor", lambda e: e.matmul(t("pCA"), t("rsb")[:, 0:128], t("carry")[:, 0:512],
                                              start=False, stop=True), waits=[mcu], label=f"carA{c}")
        mcu = op("tensor", lambda e: e.matmul(t("pCB"), t("Ub"), t("cvB"),
                                              start=True, stop=False),
                 waits=[mcu, pv_get(c, "t1b")], label=f"cumB{c}")
        mcu = op("tensor", lambda e: e.matmul(t("pCB"), t("rsb")[:, 128:256], t("carry")[:, 0:512],
                                              start=False, stop=True), waits=[mcu], label=f"carB{c}")
        mcu = op("tensor", lambda e: e.matmul(t("pCR")[:, 0:16], t("Ub"), t("crbf"),
                                              start=True, stop=False),
                 waits=[mcu, pv_get(c, "dn1"), pv_get(c, "xm2")], label=f"cumR{c}")
        s["cum"] = op("tensor", lambda e: e.matmul(t("pCR")[:, 0:16], t("rsb")[:, 256:384],
                                                   t("carry")[:, 0:16],
                                                   start=False, stop=True),
                      waits=[mcu], label=f"carR{c}")
        mto = op("tensor", lambda e: e.matmul(t("pTOT")[0:3, :], t("oh3b")[:, 0:3],
                                              t("cvA"), start=True, stop=False),
                 waits=[s["cum"], pv_get(c, "ca"), M["oh3b"], M["tot0"]], label=f"totA{c}")
        mto = op("tensor", lambda e: e.matmul(t("pTOT")[0:3, :], t("oh3b")[:, 3:6],
                                              t("cvB"), start=False, stop=False),
                 waits=[mto], label=f"totB{c}")
        s["tot"] = op("tensor", lambda e: e.matmul(t("pTOT")[0:3, 0:16], t("oh3b")[:, 6:9],
                                                   t("crbf"), start=False, stop=True),
                      waits=[mto], label=f"totR{c}")
        s["ca"] = op("vector", lambda e: e.tensor_tensor(t("carry_acc")[0:3, :],
                                                         t("carry_acc")[0:3, :],
                                                         t("pTOT")[0:3, :], ALU.add),
                     waits=[s["tot"], M["tot0"]], label=f"cacc{c}")
        s["carry"] = op("vector", lambda e: e.tensor_copy(t("carry")[0:3, :],
                                                          t("carry_acc")[0:3, :]),
                        waits=[s["ca"]], label=f"carry{c}")

    def stage_F(c):
        s = S[c]
        s["t1a"] = op("vector", lambda e: e.tensor_tensor(
            t("t1v")[:, 0:512].rearrange("p (h a) -> p h a", h=8),
            t("pCA")[:].rearrange("p (h a) -> p h a", h=8),
            t("pe2x")[:, 0:8].unsqueeze(2).to_broadcast([128, 8, 64]),
            ALU.mult), waits=[s["cum"], s["pe2x"], pv_get(c, "obf"),
                              pv_get(c, "sred")], label=f"t1a{c}")
        s["t1b"] = op("vector", lambda e: e.tensor_tensor(
            t("t1v")[:, 512:1024].rearrange("p (h a) -> p h a", h=8),
            t("pCB")[:].rearrange("p (h a) -> p h a", h=8),
            t("pe2x")[:, 8:16].unsqueeze(2).to_broadcast([128, 8, 64]),
            ALU.mult), waits=[s["t1a"]], label=f"t1b{c}")
        s["dn1"] = op("vector", lambda e: e.tensor_tensor(t("den"), t("pCR")[:, 0:16],
                                                          t("pe2x"), ALU.mult),
                      waits=[s["cum"], s["pe2x"], pv_get(c, "obf"), s["t1b"]],
                      label=f"den1{c}")
        s["dn2"] = op("vector", lambda e: e.tensor_tensor(t("dent"), t("den"),
                                                          t("dx")[:, 0:16], ALU.add),
                      waits=[s["dn1"], pv_get(c, "rden")], label=f"den2{c}")
        s["rden"] = op("vector", lambda e: e.reciprocal(t("rden"), t("dent")),
                       waits=[s["dn2"], pv_get(c, "obf")], label=f"rden{c}")
        s["nm"] = op("gpsimd", lambda e: e.tensor_tensor(t("t1v"), t("t1v"), t("t2v"),
                                                         ALU.add),
                     waits=[s["t1b"], s["t2b"]], label=f"num{c}")
        s["obf"] = op("gpsimd", lambda e: e.tensor_tensor(
            t("outbf").rearrange("p (h a) -> p h a", h=16),
            t("t1v").rearrange("p (h a) -> p h a", h=16),
            t("rden").unsqueeze(2).to_broadcast([128, 16, 64]),
            ALU.mult), waits=[s["nm"], s["rden"], pv_get(c, "oTt")],
            label=f"outbf{c}")

    def stage_G(c):
        s = S[c]
        xc = x[:, c * STR: c * STR + 1024]
        if c + 1 < NCH:
            pT_prev = S[c + 1]["hTe"]
        else:
            pT_prev = S[c - 1]["oTe"]
        m_tp2 = None
        for j in range(8):
            wj = [s["obf"], pT_prev] if j == 0 else [m_tp2]
            m_tp2 = op("tensor", lambda e, j=j: e.transpose(
                t("pT")[:, j * 128:(j + 1) * 128], t("outbf")[:, j * 128:(j + 1) * 128],
                t("idb")[:]), waits=wj, label=f"oTt{c}_{j}")
        s["oTt"] = m_tp2
        s["oTe"] = op("scalar", lambda e: e.copy(t("outT"), t("pT")[:]),
                      waits=[m_tp2, pv_get(c, "outT")], label=f"oTe{c}")
        mo = None
        for j in range(8):
            wj = [s["oTe"], pv_get(c, "xm1"), M["owfold"]] if j == 0 else [mo]
            mo = op("tensor", lambda e, j=j: e.matmul(
                t("pX")[:, 0:512], t("outT")[:, j * 128:(j + 1) * 128],
                t("OWb")[:, j * 1024: j * 1024 + 512], start=(j == 0), stop=(j == 7)),
                waits=wj, label=f"opA{c}_{j}")
            wj2 = [mo, s["dn1"]] if j == 0 else [mo]
            mo = op("tensor", lambda e, j=j: e.matmul(
                t("pCR")[:, 0:512], t("outT")[:, j * 128:(j + 1) * 128],
                t("OWb")[:, j * 1024 + 512: (j + 1) * 1024], start=(j == 0), stop=(j == 7)),
                waits=wj2, label=f"opB{c}_{j}")
        s["outT"] = mo

    def stage_Gx(c):
        s = S[c]
        xc = x[:, c * STR: c * STR + 1024]
        s["xm1"] = op("vector", lambda e, xc=xc: e.tensor_tensor(
            xc[:, 0:512], xc[:, 0:512], t("pX")[:, 0:512], ALU.add),
            waits=[s["outT"]], label=f"xm1{c}")
        s["xm2"] = op("vector", lambda e, xc=xc: e.tensor_tensor(
            xc[:, 512:1024], xc[:, 512:1024], t("pCR")[:, 0:512], ALU.add),
            waits=[s["xm1"]], label=f"xm2{c}")

    def stage_H(c):
        s = S[c]
        xc = x[:, c * STR: c * STR + 1024]
        s["mean2"] = op("vector", lambda e, xc=xc: e.reduce_sum(t("tD"), xc, axis=AX.X),
                        waits=[s["xm2"], pv_get(c, "s")], label=f"mean2{c}")
        s["negm2"] = op("vector", lambda e: e.tensor_scalar(t("tE"), t("tD"), -1.0 / D,
                                                            None, ALU.mult),
                        waits=[s["mean2"]], label=f"negm2{c}")
        s["sq2"] = op("scalar", lambda e, xc=xc: e.activation(
            t("z2a"), xc, AF.Square, bias=t("tE")[:, 0:1], accum_out=t("tG")[:, 0:1]),
            waits=[s["negm2"], pv_get(c, "sred")], label=f"sq2{c}")
        s["ln2"] = op("scalar", lambda e: e.activation(t("tG"), t("tG"), AF.Ln,
                      scale=t("kcol")[:, 1:2], bias=t("kcol")[:, 2:3]),
                      waits=[s["ln2pre"] if False else s["sq2"], M["kcol"]], label=f"lnv2{c}")
        s["rstd2"] = op("scalar", lambda e: e.activation(t("tG"), t("tG"), AF.Exp,
                        scale=t("kcol")[:, 3:4]), waits=[s["ln2"]], label=f"rstd2{c}")
        s["st0"] = op("vector", lambda e, c=c: e.tensor_copy(
            x[:, c * STR + 1024: c * STR + 1025], t("tG")),
            waits=[s["rstd2"], M["padz_all"]], label=f"st0_{c}")
        s["st1"] = op("vector", lambda e, c=c: e.tensor_copy(
            x[:, c * STR + 1025: c * STR + 1026], t("tE")),
            waits=[s["negm2"], M["padz_all"], s["st0"]], label=f"st1_{c}")
        s["z2w"] = op("vector", lambda e, xc=xc: e.tensor_scalar(
            t("z2a"), xc, t("tE")[:, 0:1], t("tG")[:, 0:1], ALU.add, ALU.mult),
            waits=[s["rstd2"], s["negm2"], s["sq2"]], label=f"z2_{c}")
        s["h2a"] = op("vector", lambda e: e.tensor_tensor(t("h2"), t("z2a"), t("CB1b"),
                                                          ALU.mult),
                      waits=[s["z2w"], M["CB1b"], M["SBb"], pv_get(c, "smul")],
                      label=f"h2a{c}")
        s["h2b"] = op("vector", lambda e: e.tensor_tensor(t("h2"), t("h2"), t("SBb"),
                                                          ALU.add),
                      waits=[s["h2a"]], label=f"h2b{c}")
        s["ss2"] = op("scalar", lambda e: e.activation(t("z2a"), t("h2"), AF.Square,
                                                       accum_out=t("tH")[:, 0:1]),
                      waits=[s["h2b"], s["z2w"]], label=f"ss2{c}")
        s["lns"] = op("scalar", lambda e: e.activation(t("tH"), t("tH"), AF.Ln,
                      bias=t("kcol")[:, 4:5]), waits=[s["ss2"], M["kcol"]], label=f"lnss{c}")
        s["r32"] = op("scalar", lambda e: e.activation(t("tH"), t("tH"), AF.Exp,
                      scale=t("kcol")[:, 3:4], bias=t("kcol")[:, 5:6]),
                      waits=[s["lns"]], label=f"r32e{c}")
        s["st2"] = op("vector", lambda e, c=c: e.tensor_copy(
            x[:, c * STR + 1026: c * STR + 1027], t("tH")),
            waits=[s["r32"], M["padz_all"], s["st1"]], label=f"st2_{c}")
        s["spill"] = dma("sync", t("d_x")[c * 128:(c + 1) * 128, :],
                         x[:, c * STR:(c + 1) * STR],
                         waits=[s["st2"]], label=f"spill{c}")
        s["smul"] = op("vector", lambda e: e.tensor_tensor(t("z2a"), t("h2"), t("Rb"),
                                                           ALU.mult),
                       waits=[s["h2b"], M["Rb"], s["ss2"]], label=f"smul{c}")
        s["sred"] = op("vector", lambda e: e.reduce_sum(t("tD"), t("z2a"), axis=AX.X),
                       waits=[s["smul"], s["negm2"]], label=f"sred{c}")
        s["s"] = op("vector", lambda e, c=c: e.tensor_tensor(
            t("s_sb")[:, c:c + 1], t("tD"), t("sbr_col"), ALU.add),
            waits=[s["sred"], M["sbr"]], label=f"s{c}")

    for c in range(NCH):
        stage_A(c)
        if c > 0:
            stage_E(c - 1)
            stage_F(c - 1)
        stage_B(c)
        stage_C(c)
        if c > 0:
            stage_G(c - 1)
        stage_D(c)
        if c > 0:
            stage_Gx(c - 1)
            stage_H(c - 1)
    stage_E(NCH - 1)
    stage_F(NCH - 1)
    stage_G(NCH - 1)
    stage_Gx(NCH - 1)
    stage_H(NCH - 1)

    s31 = S[NCH - 1]
    M["stats_all"] = b.op("vector", lambda e: e.engine_nop(),
                          waits=[S[c]["st2"] for c in range(NCH)] +
                                [S[c]["s"] for c in range(NCH)],
                          label="stats_all")
    M["spills"] = [S[c]["spill"] for c in range(NCH)]
    M["WVb_last"] = s31["vals"]
    M["OWb_last"] = s31["outT"]
    M["dx_last"] = s31["dn2"]
    M["h2_last"] = s31["smul"]
    M["z2_last"] = s31["nm"]
    M["tD_last"] = s31["sred"]
    M["t1v_last"] = s31["sred"]
    M["pT_last"] = s31["oTe"]
    M["outbf_last"] = s31["oTt"]
    M["outT_last"] = s31["outT"]
    M["hbf_last"] = s31["sred"]
    M["hT_last"] = s31["vals"]
    M["carry_last"] = s31["carry"]
    M["cvt_last"] = s31["tot"]
    M["pTOT_last"] = s31["ca"]
    b.attn_P = {"sc_free": None}
    if "x" in b.dbg:
        b.dma("sync", t("dbg_x"), t("x"), waits=[M["stats_all"]], label="dbg_x")
    if "s" in b.dbg:
        b.dma("sync", t("dbg_s"), t("s_sb"), waits=[M["stats_all"]], label="dbg_s")


def _tail(b):
    t, op, dma, dr, M = b.t, b.op, b.dma, b.dr, b.M
    pr = b.pr
    x = t("x")
    SA = M["stats_all"]

    def pad(c, k):
        return x[:, c * STR + 1024 + k: c * STR + 1024 + k + 1]

    m_iot = dma("sync", t("iotaw"), t("c_iotaw"), waits=[SA, M["p1"]], label="ld_iotaw")
    m_gld = dma("sync", t("Rb")[0:1, 0:1024], t("d_mod")[0:1, 5120:6144],
                waits=[M["h2_last"]], label="ld_gbc")
    m_gm1 = op("tensor", lambda e: e.matmul(t("pX")[:, 0:512], t("ones1f"),
                                            t("Rb")[0:1, 0:512], start=True, stop=True),
               waits=[m_gld, M["o1f"]], label="gb_mm0")
    m_ge1 = op("scalar", lambda e: e.copy(t("GBb2")[:, 0:512], t("pX")[:, 0:512]),
               waits=[m_gm1], label="gb_ev0")
    m_gm2 = op("tensor", lambda e: e.matmul(t("pX")[:, 0:512], t("ones1f"),
                                            t("Rb")[0:1, 512:1024], start=True, stop=True),
               waits=[m_ge1], label="gb_mm1")
    m_ge2 = op("scalar", lambda e: e.copy(t("GBb2")[:, 512:1024], t("pX")[:, 0:512]),
               waits=[m_gm2], label="gb_ev1")
    m_GB = op("scalar", lambda e: e.nop(), waits=[m_ge1, m_ge2], label="GBb2")
    m_l2 = dma("sync", t("h2")[0:1, 0:1024], t("lb2"), waits=[M["h2_last"]], label="ld_lb2")
    m_l2g = op("vector", lambda e: e.tensor_tensor(t("lb2gb"), t("h2")[0:1, 0:1024],
                                                   t("GBb2")[0:1, 0:1024], ALU.mult),
               waits=[m_l2, m_GB, M["carry_last"]], label="lb2gb")
    m_h2r = dma("sync", t("h2")[0:1, 0:1024], t("hb2"), waits=[m_l2g], label="ld_hb2")
    m_h2g = op("vector", lambda e: e.tensor_tensor(t("hb2gb"), t("h2")[0:1, 0:1024],
                                                   t("GBb2")[0:1, 0:1024], ALU.mult),
               waits=[m_h2r, m_GB], label="hb2gb")
    m_lb1 = dma("gpsimd", t("lb1b"), t("lb1"), waits=[M["p2"], SA], label="ld_lb1")
    m_lgF = dma("sync", t("dx")[:, 0:8], t("lgF"), waits=[M["dx_last"]], label="ld_lgF")
    m_hgF = dma("sync", t("dx")[:, 8:16], t("hgF"), waits=[m_lgF], label="ld_hgF")
    m_lw1 = dma("gpsimd", t("v_lw1b").rearrange("p (s o) -> p s o", s=8),
                t("lw1").rearrange("(s p) o -> p s o", p=128),
                waits=[M["WVb_last"]], label="ld_lw1")
    m_lw2 = dma("gpsimd", t("v_lw2b").rearrange("p (s o) -> p s o", s=4),
                t("lw2").rearrange("(s p) o -> p s o", p=128),
                waits=[m_lw1], label="ld_lw2")
    m_lf = None
    for s in range(8):
        m_lf = op("vector", lambda e, s=s: e.tensor_scalar(
            t("v_lw1b")[:, s * 512:(s + 1) * 512], t("v_lw1b")[:, s * 512:(s + 1) * 512],
            t("dx")[:, s:s + 1], None, ALU.mult),
            waits=[m_lw1, m_lgF] + ([m_lf] if m_lf else []), label=f"lw1f{s}")
    for s in range(4):
        m_lf = op("vector", lambda e, s=s: e.tensor_tensor(
            t("v_lw2b")[:, s * 1024:(s + 1) * 1024], t("v_lw2b")[:, s * 1024:(s + 1) * 1024],
            t("GBb2"), ALU.mult),
            waits=[m_lw2, m_GB, m_lf], label=f"lw2f{s}")
    M["lwf"] = m_lf

    chain = []

    class Ser:
        def __init__(self):
            self.pend = []

        def add(self, engine, fn, waits=(), label=""):
            self.pend.append((engine, fn, list(waits), label))

        def emit(self, k):
            for _ in range(k):
                if not self.pend:
                    return
                engine, fn, waits, label = self.pend.pop(0)
                self.last = op(engine, fn, waits=waits + ([self.last] if hasattr(self, "last") else []),
                               label=label)
            return self.last

    M["onesq"] = dma("sync", t("onesq"), t("c_ones2"),
                     waits=[M["OWb_last"]], label="ld_onesq")
    M["onesqb"] = dma("gpsimd", t("onesqb"), t("c_ones2"),
                      waits=[M["OWb_last"]], label="ld_onesqb")
    m_mn = op("vector", lambda e: e.tensor_scalar(t("tF"), t("s_sb"), 64.0, None, ALU.min),
              waits=[SA, M["p1"]], label="es_min")
    m_es = op("scalar", lambda e: e.activation(t("es"), t("tF"), AF.Exp),
              waits=[m_mn], label="es_exp")
    M["es"] = m_es
    if "es" in b.dbg:
        dma("sync", t("dbg_es"), t("es"), waits=[m_es], label="dbg_es")

    def _red_bcnt(e, src):
        with b.nc.allow_low_precision(reason="integer chunk counts, exact in bf16"):
            return e.reduce_sum(t("bcnt"), src, axis=AX.X)

    ser = Ser()
    A = ser.add
    bst = t("bst")
    A("vector", lambda e: e.memset(bst[:, 5:6], 4096.0 / 1152.0),
      waits=[M["tD_last"], m_ge2], label="T0")
    for i in range(NITER):
        A("vector", lambda e: e.tensor_single_scalar(t("tF"), t("es"), bst[:, 5:6], ALU.min),
          waits=[m_es], label=f"Tmin{i}")
        A("vector", lambda e: e.reduce_sum(bst[:, 6:7], t("tF"), axis=AX.X), label=f"Tred{i}")
        A("tensor", lambda e: e.matmul(t("pX")[:, 4:5], t("onesq"), bst[:, 6:7],
                                       start=True, stop=True),
          waits=[M["onesq"]], label=f"Tsum{i}")
        A("vector", lambda e: e.tensor_scalar(bst[:, 5:6], t("pX")[:, 4:5],
                                              1.0 / 1152.0, None, ALU.mult), label=f"Tscl{i}")
    A("vector", lambda e: e.tensor_single_scalar(t("tF2"), t("es"), bst[:, 5:6], ALU.min),
      label="u")
    A("vector", lambda e: e.memset(bst[:, 0:1], 0.0), label="blo0")
    A("vector", lambda e: e.tensor_scalar(bst[:, 1:2], bst[:, 5:6], 2.0, None,
                                          ALU.mult), label="bhi0")
    for i in range(NBIS):
        A("vector", lambda e: e.tensor_tensor(bst[:, 2:3], bst[:, 0:1], bst[:, 1:2],
                                              ALU.add), label=f"ba{i}")
        A("vector", lambda e: e.tensor_scalar(bst[:, 2:3], bst[:, 2:3], 0.5, None,
                                              ALU.mult), label=f"bb{i}")
        A("vector", lambda e: e.tensor_single_scalar(t("msup"), t("tF2"), bst[:, 2:3],
                                                     ALU.is_ge), label=f"bc{i}")
        A("vector", lambda e: _red_bcnt(e, t("msup")), label=f"bd{i}")
        A("tensor", lambda e: e.matmul(t("pX")[:, 4:5], t("onesqb"), t("bcnt"),
                                       start=True, stop=True),
          waits=[M["onesqb"]], label=f"be{i}")
        A("vector", lambda e: e.tensor_scalar(bst[:, 3:4], t("pX")[:, 4:5], 1024.0, None,
                                              ALU.is_ge), label=f"bf{i}")
        A("vector", lambda e: e.tensor_tensor(bst[:, 4:5], bst[:, 2:3], bst[:, 0:1],
                                              ALU.subtract), label=f"bg{i}")
        A("vector", lambda e: e.scalar_tensor_tensor(bst[:, 0:1], bst[:, 4:5], bst[:, 3:4],
                                                     bst[:, 0:1], ALU.mult, ALU.add),
          label=f"bh{i}")
        A("vector", lambda e: e.tensor_tensor(bst[:, 4:5], bst[:, 1:2], bst[:, 2:3],
                                              ALU.subtract), label=f"bi{i}")
        A("vector", lambda e: e.scalar_tensor_tensor(bst[:, 1:2], bst[:, 4:5], bst[:, 3:4],
                                                     bst[:, 2:3], ALU.mult, ALU.add),
          label=f"bj{i}")

    def mask_build(qval, mdst, tag):
        A("vector", lambda e: e.tensor_single_scalar(t("gehi"), t("tF2"), bst[:, 1:2],
                                                     ALU.is_ge), label=f"{tag}ge")
        A("vector", lambda e: e.tensor_single_scalar(t("tF3"), t("tF2"), bst[:, 0:1],
                                                     ALU.is_ge), label=f"{tag}gl")
        A("vector", lambda e: e.tensor_tensor(t("eqm"), t("tF3"), t("gehi"), ALU.subtract),
          label=f"{tag}eq")
        A("vector", lambda e: _red_bcnt(e, t("gehi")), label=f"{tag}r")
        A("tensor", lambda e: e.matmul(t("pX")[:, 4:5], t("onesqb"), t("bcnt"),
                                       start=True, stop=True), label=f"{tag}s")
        A("vector", lambda e, qval=qval: e.tensor_scalar(bst[:, 7:8], t("pX")[:, 4:5],
                                                         -1.0, qval, ALU.mult, ALU.add),
          label=f"{tag}q")
        A("vector", lambda e: e.tensor_copy(t("eqb"), t("eqm")), label=f"{tag}eb")
        A("tensor", lambda e: e.matmul(t("pX")[0:1, 64:96], t("Ub")[:, 127:128], t("eqb"),
                                       start=True, stop=True), waits=[M["Ub"]], label=f"{tag}t")
        A("vector", lambda e: e.tensor_copy(t("rowP"), t("pX")[0:1, 64:96]), label=f"{tag}tp")
        for sft in (1, 2, 4, 8, 16):
            A("vector", lambda e, sft=sft: e.tensor_tensor(
                t("rowQ")[:, sft:NCH], t("rowP")[:, sft:NCH], t("rowP")[:, 0:NCH - sft],
                ALU.add), label=f"{tag}p{sft}")
            A("vector", lambda e, sft=sft: e.tensor_copy(t("rowQ")[:, 0:sft], t("rowP")[:, 0:sft]),
              label=f"{tag}pc{sft}")
            A("vector", lambda e: e.tensor_copy(t("rowP"), t("rowQ")), label=f"{tag}px{sft}")
        A("vector", lambda e: e.tensor_tensor(t("rowP"), t("rowP"), t("pX")[0:1, 64:96],
                                              ALU.subtract), label=f"{tag}ex")
        A("tensor", lambda e: e.matmul(t("pX")[:, 32:64], t("Ub"), t("eqb"),
                                       start=True, stop=False), label=f"{tag}u")
        A("tensor", lambda e: e.matmul(t("pX")[:, 32:64], t("ones1f"), t("rowP")[0:1, 0:NCH],
                                       start=False, stop=True), label=f"{tag}ou")
        A("vector", lambda e: e.tensor_tensor(t("tF3"), t("pX")[:, 32:64], t("eqm"),
                                              ALU.subtract), label=f"{tag}rx")
        A("vector", lambda e: e.tensor_single_scalar(t("tF3"), t("tF3"), bst[:, 7:8],
                                                     ALU.is_lt), label=f"{tag}lt")
        A("vector", lambda e: e.tensor_tensor(t("tF3"), t("tF3"), t("eqm"), ALU.mult),
          label=f"{tag}ml")
        A("vector", lambda e, mdst=mdst: e.tensor_tensor(t(mdst), t("gehi"), t("tF3"), ALU.add),
          label=f"{tag}m")

    mask_build(1024.0, "msk", "m1")
    A("vector", lambda e: e.tensor_copy(t("supb"), t("msk")), label="supb")
    A("tensor", lambda e: e.matmul(t("pX")[0:1, 64:96], t("Ub")[:, 127:128], t("supb"),
                                   start=True, stop=True), label="slT")
    A("vector", lambda e: e.tensor_copy(t("rowP"), t("pX")[0:1, 64:96]), label="slTp")
    for sft in (1, 2, 4, 8, 16):
        A("vector", lambda e, sft=sft: e.tensor_tensor(
            t("rowQ")[:, sft:NCH], t("rowP")[:, sft:NCH], t("rowP")[:, 0:NCH - sft],
            ALU.add), label=f"sp{sft}")
        A("vector", lambda e, sft=sft: e.tensor_copy(t("rowQ")[:, 0:sft], t("rowP")[:, 0:sft]),
          label=f"spc{sft}")
        A("vector", lambda e: e.tensor_copy(t("rowP"), t("rowQ")), label=f"spx{sft}")
    A("vector", lambda e: e.tensor_tensor(t("rowP"), t("rowP"), t("pX")[0:1, 64:96],
                                          ALU.subtract), label="slex")
    A("tensor", lambda e: e.matmul(t("pX")[:, 32:64], t("Ub"), t("supb"),
                                   start=True, stop=False), label="slU")
    A("tensor", lambda e: e.matmul(t("pX")[:, 32:64], t("ones1f"), t("rowP")[0:1, 0:NCH],
                                   start=False, stop=True), label="slou")
    A("vector", lambda e: e.tensor_tensor(t("tF3"), t("pX")[:, 32:64], t("msk"),
                                          ALU.subtract), label="slrx")
    A("vector", lambda e: e.scalar_tensor_tensor(t("tF"), t("msk"), -BIGF, t("tF3"),
                                                 ALU.mult, ALU.add), label="slbig")
    A("vector", lambda e: e.tensor_scalar(t("tF"), t("tF"), BIGF, None, ALU.add), label="slbig2")
    A("vector", lambda e: e.tensor_copy(t("slot_i32"), t("tF")), label="slcast")
    m_p3_done_placeholder = None

    PL = {"sc_free": None,
          "z2_prev0": M["t1v_last"], "z2_prev1": M["z2_last"],
          "hbf_free": M["hbf_last"], "pT_free": M["pT_last"],
          "hT_free": M["hT_last"], "outbf_free": M["outbf_last"], "h2_free": m_h2g,
          "outT_free": M["outT_last"], "w1_free": None, "w2_free": None,
          "gelu_free": None}
    st_y = []
    nsteps = (len(ser.pend) + 23) // 24
    m_p3 = None
    m_hw1 = []
    m_hf = None
    for c in range(NCH):
        m_p3 = ser.emit(nsteps) or m_p3
        xc = x[:, c * STR: c * STR + 1024]
        zL = t("t1v") if c % 2 == 0 else t("t2v")
        zkey = "z2_prev0" if c % 2 == 0 else "z2_prev1"
        m_sc = op("vector", lambda e, c=c: e.tensor_tensor(t("den")[:, 0:1], pad(c, 0), pad(c, 2),
                                                           ALU.mult),
                  waits=[SA, PL["sc_free"]] + ([M["dx_last"]] if c == 0 else []), label=f"LBq{c}")
        m_z = op("vector", lambda e, xc=xc, zL=zL, c=c: e.tensor_scalar(
            zL, xc, pad(c, 1), t("den")[:, 0:1], ALU.add, ALU.mult),
            waits=[m_sc, PL[zkey]], label=f"Lz{c}")
        m_h2 = op("vector", lambda e, zL=zL: e.tensor_tensor(t("h2"), zL, t("CB1b"), ALU.mult),
                  waits=[m_z, PL["h2_free"]], label=f"Lh2{c}")
        m_hs = op("vector", lambda e, c=c: e.scalar_tensor_tensor(
            t("hbf"), t("SBb"), pad(c, 2), t("h2"), ALU.mult, ALU.add),
            waits=[m_h2, PL["hbf_free"]], label=f"Lhs{c}")
        m_tp = None
        for j in range(8):
            wj = [m_hs, PL["pT_free"]] if j == 0 else [m_tp]
            m_tp = op("tensor", lambda e, j=j: e.transpose(
                t("pT")[:, j * 128:(j + 1) * 128], t("hbf")[:, j * 128:(j + 1) * 128],
                t("idb")[:]), waits=wj, label=f"LT{c}_{j}")
        m_hbf_free = m_tp
        m_hTe = op("scalar", lambda e: e.copy(t("hT"), t("pT")), waits=[m_tp, PL["hT_free"]],
                   label=f"LTe{c}")
        m_pT_free1 = m_hTe
        mw = None
        for j in range(8):
            wj = [m_hTe, M["lwf"], PL["w1_free"]] if j == 0 else [mw]
            mw = op("tensor", lambda e, j=j: e.matmul(
                t("pVA"), t("hT")[:, j * 128:(j + 1) * 128],
                t("v_lw1b")[:, j * 512:(j + 1) * 512], start=(j == 0), stop=False),
                waits=wj, label=f"Lw1_{c}_{j}")
        mw = op("tensor", lambda e: e.matmul(t("pVA"), t("ones1b"), t("lb1b"),
                                             start=False, stop=True),
                waits=[mw, m_lb1], label=f"Lb1{c}")
        m_gel = op("scalar", lambda e: e.activation(t("outbf")[:, 0:512], t("pVA"), AF.Gelu),
                   waits=[mw, PL["gelu_free"], PL["outbf_free"]], label=f"Lg{c}")
        m_w1_free = m_gel
        m_tp2 = None
        for j in range(4):
            wj = [m_gel, m_hTe] if j == 0 else [m_tp2]
            m_tp2 = op("tensor", lambda e, j=j: e.transpose(
                t("pT")[:, j * 128:(j + 1) * 128], t("outbf")[:, j * 128:(j + 1) * 128],
                t("idb")[:]), waits=wj, label=f"LT2{c}_{j}")
        m_outbf_free = m_tp2
        m_oTe = op("scalar", lambda e: e.copy(t("outT")[:, 0:512], t("pT")[:, 0:512]),
                   waits=[m_tp2, PL["outT_free"]], label=f"LTe2{c}")
        m_pT_free = m_oTe
        mw2 = None
        for j in range(4):
            wj = [m_oTe, M["lwf"], PL["w2_free"]] if j == 0 else [mw2]
            mw2 = op("tensor", lambda e, j=j: e.matmul(
                t("pCA"), t("outT")[:, j * 128:(j + 1) * 128],
                t("v_lw2b")[:, j * 1024: j * 1024 + 512], start=(j == 0), stop=False),
                waits=wj, label=f"Lw2A{c}_{j}")
            mw2 = op("tensor", lambda e, j=j: e.matmul(
                t("pCB"), t("outT")[:, j * 128:(j + 1) * 128],
                t("v_lw2b")[:, j * 1024 + 512:(j + 1) * 1024], start=(j == 0), stop=False),
                waits=[mw2], label=f"Lw2B{c}_{j}")
        mw2 = op("tensor", lambda e: e.matmul(t("pCA"), t("ones1b"), t("lb2gb")[:, 0:512],
                                              start=False, stop=True),
                 waits=[mw2, m_l2g], label=f"Lb2A{c}")
        mw2 = op("tensor", lambda e: e.matmul(t("pCB"), t("ones1b"), t("lb2gb")[:, 512:1024],
                                              start=False, stop=True),
                 waits=[mw2], label=f"Lb2B{c}")
        m_xo1 = op("vector", lambda e, xc=xc, zL=zL: e.tensor_tensor(
            zL[:, 0:512], xc[:, 0:512], t("pCA"), ALU.add),
            waits=[mw2, m_h2], label=f"Lxo1{c}")
        m_xo2 = op("vector", lambda e, xc=xc, zL=zL: e.tensor_tensor(
            zL[:, 512:1024], xc[:, 512:1024], t("pCB"), ALU.add),
            waits=[m_xo1], label=f"Lxo2{c}")
        m_w2_free = m_xo2
        m_st = dma("sync", t("y")[c * 128:(c + 1) * 128, :], zL,
                   waits=[m_xo2], label=f"sty{c}")
        st_y.append(m_st)
        PL.update({"sc_free": m_z, zkey: m_st, "hbf_free": m_hbf_free, "pT_free": m_pT_free,
                   "hT_free": mw, "outbf_free": m_outbf_free, "h2_free": m_hs,
                   "outT_free": mw2, "w1_free": m_gel, "w2_free": m_xo2,
                   "gelu_free": m_tp2})
        if c == 15:
            hw1_w = st_y[0:16] + M["spills"][0:16]
            for p in range(4):
                m_hw1.append(dma(
                    "gpsimd",
                    t("v_hw1b").rearrange("p (s o) -> p s o", s=8)[:, :, p * 1024:(p + 1) * 1024],
                    t("hw1").rearrange("(s p) o -> p s o", p=128)[:, :, p * 1024:(p + 1) * 1024],
                    waits=hw1_w, label=f"ld_hw1_{p}"))
        if 17 <= c <= 24:
            s_ = c - 17
            m_hf = op("vector", lambda e, s=s_: e.tensor_scalar(
                t("v_hw1b")[:, s * 4096:(s + 1) * 4096], t("v_hw1b")[:, s * 4096:(s + 1) * 4096],
                t("dx")[:, 8 + s: 9 + s], None, ALU.mult),
                waits=m_hw1 + [m_hgF] + ([m_hf] if m_hf else []), label=f"hw1f{s_}")
    M["hw1f"] = m_hf
    m_p3 = ser.emit(10 ** 9) or m_p3
    M["p3"] = m_p3
    if "mask" in b.dbg:
        dma("sync", t("dbg_mask"), t("msk"), waits=[m_p3], label="dbg_mask")
    if "msup" in b.dbg:
        dma("sync", t("dbg_msup"), t("msup"), waits=[m_p3], label="dbg_msup")
    if "slots" in b.dbg:
        dma("sync", t("dbg_slots"), t("tF"), waits=[m_p3], label="dbg_slots")
    if "u" in b.dbg:
        dma("sync", t("dbg_u"), t("tF2"), waits=[m_p3], label="dbg_u")

    hw2_w = st_y[16:32] + M["spills"][16:32]
    m_hw2 = []
    for p in range(4):
        m_hw2.append(dma("gpsimd",
                         t("v_hw2b")[:, p * 8192:(p + 1) * 8192].rearrange("p (s o) -> p s o", s=8),
                         t("hw2")[p * 1024:(p + 1) * 1024, :].rearrange("(s p) o -> p s o", p=128),
                         waits=hw2_w, label=f"ld_hw2_{p}"))
    hw2f_mark = []
    for s in range(32):
        m_hf = op("vector", lambda e, s=s: e.tensor_tensor(
            t("v_hw2b")[:, s * 1024:(s + 1) * 1024], t("v_hw2b")[:, s * 1024:(s + 1) * 1024],
            t("GBb2"), ALU.mult),
            waits=[m_hw2[s // 8], m_GB, m_hf], label=f"hw2f{s}")
        if s % 8 == 7:
            hw2f_mark.append(m_hf)
    M["hwf"] = m_hf

    for c in range(NCH):
        q = ("qG", "qG2", "qG3")[c % 3]
        m2 = b.pr.raw("gpsimd", lambda e, c=c: e.indirect_dma_start(
            t("d_goff"), IOA(t("slot_i32")[:, c:c + 1], 0),
            t("iotaw")[:, c * 8:(c + 1) * 8], None,
            bounds_check=b.bcreg, oob_is_err=False),
            waits=[m_iot, M["p3"], b._lastdma.get(q)], qsem=q, label=f"scg{c}")
        b._lastdma[q] = m2
    m_scg = b.pr.raw("gpsimd", lambda e: e.nop(),
                     waits=[b._lastdma.get(qq) for qq in ("qG", "qG2", "qG3")],
                     label="scg_all")
    m_goff = dma("sync", t("goff").rearrange("p (g k) -> p g k", g=NG),
                 t("d_goff").rearrange("(g p) k -> p g k", p=128),
                 waits=[m_scg], label="ld_goff")
    m_hb1a = dma("gpsimd", t("hb1bA"), t("hb1")[0:1, 0:2048],
                 waits=[m_hf, M["lwf"]], label="ld_hb1a")
    m_hb1b = dma("gpsimd", t("hb1bB"), t("hb1")[0:1, 2048:4096],
                 waits=[m_hb1a, st_y[30]], label="ld_hb1b")

    def hb1slice(p):
        if p < 2:
            return t("hb1bA")[:, p * 1024:(p + 1) * 1024]
        return t("hb1bB")[:, (p - 2) * 1024:(p - 2 + 1) * 1024]

    pTOTb = t("pTOT")[:].bitcast(BF16)
    xg_t = [t("v_xg"), t("v_xg2")]
    hT_t = [t("hT"), t("hT2")]
    gob_t = [t("v_gob"), t("outbf")]
    outT_t = [t("outT"), t("cvt")]
    w1ps = [(t("pVA")[:], t("pVB")[:]), (t("pX")[:, 0:512], t("pCR")[:, 0:512])]
    PH = {"xg_free": [None, None],
          "hT_free": [PL["hT_free"], PL["outT_free"]],
          "w1_free": [None, None],
          "gob_free": [M["OWb_last"], PL["outbf_free"]],
          "outT_free": [PL["outT_free"], M["cvt_last"]],
          "pT_free": PL["pT_free"], "pTOTb_free": M["pTOT_last"],
          "hbf_free": PL["hbf_free"],
          "z2_prev": st_y[31], "h2_free": PL["h2_free"],
          "hfA_rd": None, "hfB_rd": None, "sc_free": None}
    y_all = [("qS", pr.val("qS")), ("qS2", pr.val("qS2")), ("qS3", pr.val("qS3"))]
    for g in range(NG):
        gp = g % 2
        xg = xg_t[gp]
        hT = hT_t[gp]
        hf = "v_hfA" if g % 2 == 0 else "v_hfB"
        hf_rd = "hfA_rd" if g % 2 == 0 else "hfB_rd"
        qx = ("qG", "qG2", "qG3")[g % 3]
        xgw = [m_goff, PH["xg_free"][gp], b._lastdma.get(qx)] + M["spills"]
        if g == 0:
            xgw.append(M["OWb_last"])
        if g == 1:
            xgw.append(PL["hT_free"])
        m_xg = b.pr.raw("gpsimd", lambda e, g=g, xg=xg: e.indirect_dma_start(
            xg, None, t("d_x"), IOA(t("goff")[:, g * 8: g * 8 + 1], 0)),
            waits=xgw, qsem=qx, label=f"ldxg{g}")
        b._lastdma[qx] = m_xg
        m_sc = op("vector", lambda e, xg=xg: e.tensor_tensor(
            t("rden")[:, 0:1], xg[:, 1024:1025], xg[:, 1026:1027], ALU.mult),
            waits=[m_xg, PH["sc_free"]], label=f"HBq{g}")
        m_z = op("vector", lambda e, xg=xg: e.tensor_scalar(
            t("z2"), xg[:, 0:1024], xg[:, 1025:1026], t("rden")[:, 0:1],
            ALU.add, ALU.mult), waits=[m_sc, PH["z2_prev"]], label=f"Hz{g}")
        m_h2 = op("vector", lambda e: e.tensor_tensor(t("h2"), t("z2"), t("CB1b"), ALU.mult),
                  waits=[m_z, PH["h2_free"]], label=f"Hh2{g}")
        m_hs = op("vector", lambda e, xg=xg: e.scalar_tensor_tensor(
            t("hbf"), t("SBb"), xg[:, 1026:1027], t("h2"), ALU.mult, ALU.add),
            waits=[m_h2, PH["hbf_free"]], label=f"Hhs{g}")
        m_tp = None
        for j in range(8):
            wj = [m_hs, PH["pTOTb_free"]] if j == 0 else [m_tp]
            m_tp = op("tensor", lambda e, j=j: e.transpose(
                pTOTb[:, j * 128:(j + 1) * 128], t("hbf")[:, j * 128:(j + 1) * 128],
                t("idb")[:]), waits=wj, label=f"HT{g}_{j}")
        m_hTe = op("scalar", lambda e, hT=hT: e.copy(hT, pTOTb),
                   waits=[m_tp, PH["hT_free"][gp]], label=f"HTe{g}")
        PH["hbf_free"] = m_tp
        PH["z2_prev"] = m_h2
        PH["h2_free"] = m_hs
        PH["sc_free"] = m_z
        PH["xg_free"][gp] = m_hs
        PH["pTOTb_free"] = m_hTe
        mw2last = None
        marks = {}

        def front(p):
            pp = p % 2
            wA, wB = w1ps[pp]
            mw = None
            for j in range(8):
                wj = [m_hTe, PH["w1_free"][pp]] if j == 0 else [mw]
                if g == 0 and j == 0:
                    wj.append(M["hw1f"])
                mw = op("tensor", lambda e, j=j, p=p, wA=wA, hT=hT: e.matmul(
                    wA, hT[:, j * 128:(j + 1) * 128],
                    t("v_hw1b")[:, j * 4096 + p * 1024: j * 4096 + p * 1024 + 512],
                    start=(j == 0), stop=False),
                    waits=[w for w in wj if w], label=f"Hw1A{g}_{p}_{j}")
                mw = op("tensor", lambda e, j=j, p=p, wB=wB, hT=hT: e.matmul(
                    wB, hT[:, j * 128:(j + 1) * 128],
                    t("v_hw1b")[:, j * 4096 + p * 1024 + 512: j * 4096 + (p + 1) * 1024],
                    start=(j == 0), stop=False), waits=[mw], label=f"Hw1B{g}_{p}_{j}")
            mw = op("tensor", lambda e, p=p, wA=wA: e.matmul(
                wA, t("ones1b"), hb1slice(p)[:, 0:512], start=False, stop=True),
                waits=[mw, m_hb1b], label=f"Hb1A{g}_{p}")
            mw = op("tensor", lambda e, p=p, wB=wB: e.matmul(
                wB, t("ones1b"), hb1slice(p)[:, 512:1024], start=False, stop=True),
                waits=[mw], label=f"Hb1B{g}_{p}")
            marks[("w1", p)] = mw
            gob = gob_t[pp]
            m_gA = op("scalar", lambda e, gob=gob, wA=wA: e.activation(
                gob[:, 0:512], wA, AF.Gelu),
                waits=[mw, PH["gob_free"][pp]], label=f"HgA{g}_{p}")
            m_gB = op("scalar", lambda e, gob=gob, wB=wB: e.activation(
                gob[:, 512:1024], wB, AF.Gelu), waits=[m_gA], label=f"HgB{g}_{p}")
            marks[("gel", p)] = m_gB
            PH["w1_free"][pp] = m_gB

        def back(p, mw2last):
            pp = p % 2
            gob = gob_t[pp]
            trg = t("pT")[:] if pp == 0 else pTOTb
            oT = outT_t[pp]
            m_tp2 = None
            for j in range(8):
                if j == 0:
                    wj = [marks[("gel", p)],
                          PH["pT_free"] if pp == 0 else PH["pTOTb_free"]]
                else:
                    wj = [m_tp2]
                m_tp2 = op("tensor", lambda e, j=j, gob=gob, trg=trg: e.transpose(
                    trg[:, j * 128:(j + 1) * 128], gob[:, j * 128:(j + 1) * 128],
                    t("idb")[:]), waits=wj, label=f"HT2{g}_{p}_{j}")
            PH["gob_free"][pp] = m_tp2
            m_oTe = op("vector", lambda e, oT=oT, trg=trg: e.tensor_copy(oT, trg),
                       waits=[m_tp2, PH["outT_free"][pp]], label=f"HTe2{g}_{p}")
            if pp == 0:
                PH["pT_free"] = m_oTe
            else:
                PH["pTOTb_free"] = m_oTe
            for j in range(8):
                wj = [m_oTe]
                if p == 0 and j == 0:
                    wj.append(PH[hf_rd])
                if g == 0 and j == 0:
                    wj.append(hw2f_mark[p])
                if mw2last is not None:
                    wj.append(mw2last)
                mw2last = op("tensor", lambda e, j=j, p=p, oT=oT: e.matmul(
                    t("pCA"), oT[:, j * 128:(j + 1) * 128],
                    t("v_hw2b")[:, (8 * p + j) * 1024: (8 * p + j) * 1024 + 512],
                    start=(p == 0 and j == 0), stop=False),
                    waits=[w for w in wj if w], label=f"Hw2A{g}_{p}_{j}")
                mw2last = op("tensor", lambda e, j=j, p=p, oT=oT: e.matmul(
                    t("pCB"), oT[:, j * 128:(j + 1) * 128],
                    t("v_hw2b")[:, (8 * p + j) * 1024 + 512: (8 * p + j + 1) * 1024],
                    start=(p == 0 and j == 0), stop=False),
                    waits=[mw2last], label=f"Hw2B{g}_{p}_{j}")
            PH["outT_free"][pp] = mw2last
            return mw2last

        for p in range(4):
            front(p)
            if p > 0:
                mw2last = back(p - 1, mw2last)
        mw2last = back(3, mw2last)
        PH["hT_free"][gp] = marks[("w1", 3)]
        mw2last = op("tensor", lambda e: e.matmul(t("pCA"), t("ones1b"), t("hb2gb")[:, 0:512],
                                                  start=False, stop=True),
                     waits=[mw2last, m_h2g], label=f"Hb2A{g}")
        mw2last = op("tensor", lambda e: e.matmul(t("pCB"), t("ones1b"), t("hb2gb")[:, 512:1024],
                                                  start=False, stop=True),
                     waits=[mw2last], label=f"Hb2B{g}")
        m_ev1 = op("vector", lambda e, hf=hf: e.tensor_copy(t(hf)[:, 0:512], t("pCA")),
                   waits=[mw2last, PH[hf_rd]], label=f"Hev1{g}")
        m_ev2 = op("vector", lambda e, hf=hf: e.tensor_copy(t(hf)[:, 512:1024], t("pCB")),
                   waits=[m_ev1], label=f"Hev2{g}")
        qp = ("qG", "qG2", "qG3")[g % 3]
        m_p6 = b.pr.raw("gpsimd", lambda e, hf=hf, g=g: e.indirect_dma_start(
            t("y"), IOA(t("goff")[:, g * 8: g * 8 + 1], 0), t(hf), None,
            compute_op=ALU.add),
            waits=[m_ev2, m_goff, b._lastdma.get(qp)] + y_all, qsem=qp,
            label=f"p6_{g}")
        b._lastdma[qp] = m_p6
        PH[hf_rd] = m_p6

    b.pr.raw("sync", lambda e: e.nop(),
             waits=[(q, b.pr.val(q)) for q in
                    ("qS", "qS2", "qS3", "qG", "qG2", "qG3")],
             incs=[], label="final_wait")


def _host_inputs(inputs, core):
    f = np.ascontiguousarray
    vw = np.asarray(inputs["L0_attn_value_weight"], np.float32)
    ow = np.asarray(inputs["L0_attn_output_weight"], np.float32)
    k1 = np.asarray(inputs["L0_attn_k1"], np.float32)
    k2 = np.asarray(inputs["L0_attn_k2"], np.float32)
    k3 = np.asarray(inputs["L0_attn_k3"], np.float32)
    WV = vw.transpose(1, 0, 2).reshape(D, H * 64)
    OW = ow.transpose(0, 2, 1).reshape(H * 64, D)
    K123 = np.concatenate([k1.T, k2.T, k3.T], axis=1)

    def fmaj(v):
        return f(np.asarray(v, np.float32).reshape(8, 128).T)

    def hpf(v):
        v = np.asarray(v, np.float32).reshape(H, 64)
        out = np.empty((128, 8), np.float32)
        for k in range(8):
            out[0:64, k] = v[2 * k]
            out[64:128, k] = v[2 * k + 1]
        return out

    trigF = np.concatenate([hpf(inputs["L0_attn_a1"]), hpf(inputs["L0_attn_b1"]),
                            hpf(inputs["L0_attn_a2"]), hpf(inputs["L0_attn_b2"]),
                            hpf(inputs["L0_attn_c"])], axis=1)
    r = (np.arange(128, dtype=np.float64) / (N - 1)).astype(np.float32)
    cg = ((128.0 * np.arange(NCH, dtype=np.float64)) / (N - 1)).astype(np.float32)
    rs = np.zeros((4, 3 * 128), np.float32)
    for j in range(3):
        rs[j, j * 128:(j + 1) * 128] = 1.0
    oh3 = np.zeros((128, 9), np.float32)
    for j in range(3):
        oh3[:, j * 3 + j] = 1.0
    tok = (np.arange(NCH)[None, :] * 128 + np.arange(128)[:, None]).astype(np.int32)
    return {
        "x_in": f(np.asarray(inputs["L0_attn_x"], np.float32)[core]),
        "cvecF": fmaj(np.asarray(inputs["L0_c"], np.float32)[core]),
        "wmod": f(np.asarray(inputs["L0_wmod"], np.float32)),
        "WV": f(WV), "OW": f(OW), "K123": f(K123),
        "trigF": f(trigF),
        "rtok": f(np.asarray(inputs["L0_ffn_routing_token"], np.float32)[None, :]),
        "lgF": fmaj(inputs["L0_ffn_light_gamma"]),
        "hgF": fmaj(inputs["L0_ffn_heavy_gamma"]),
        "lw1": f(np.asarray(inputs["L0_ffn_light_w1"], np.float32)),
        "lw2": f(np.asarray(inputs["L0_ffn_light_w2"], np.float32)),
        "hw1": f(np.asarray(inputs["L0_ffn_heavy_w1"], np.float32)),
        "hw2": f(np.asarray(inputs["L0_ffn_heavy_w2"], np.float32)),
        "lb1": f(np.asarray(inputs["L0_ffn_light_b1"], np.float32)[None, :]),
        "lb2": f(np.asarray(inputs["L0_ffn_light_b2"], np.float32)[None, :]),
        "hb1": f(np.asarray(inputs["L0_ffn_heavy_b1"], np.float32)[None, :]),
        "hb2": f(np.asarray(inputs["L0_ffn_heavy_b2"], np.float32)[None, :]),
        "c_U": f(np.triu(np.ones((128, 128), np.float32))),
        "c_Us": f(np.triu(np.ones((128, 128), np.float32), 1)),
        "c_id": f(np.eye(128, dtype=np.float32)),
        "c_ones1": f(np.ones((1, 128), np.float32)),
        "c_ones2": f(np.ones((128, 128), np.float32)),
        "c_rs": f(rs),
        "c_oh3": f(oh3),
        "c_rrow": f(np.broadcast_to(r, (128, 128)).copy()),
        "c_ncrow": f(np.broadcast_to(cg, (128, NCH)).copy()),
        "c_iotaw": f(np.repeat(tok[:, :, None], 8, axis=2).reshape(128, NCH * 8)),
    }


def kernel(**inputs):
    nc = build()
    in_maps = [_host_inputs(inputs, core) for core in range(B)]
    res = run_bass_kernel_spmd(nc, in_maps, core_ids=list(range(B)))
    out = np.stack([np.asarray(res.results[i]["y"]) for i in range(B)], axis=0)
    return out.astype(np.float32)


# version 31 (final)
# speedup vs baseline: 1.0188x; 1.0188x over previous
import numpy as np
from contextlib import ExitStack

import concourse.bass as bass
import concourse.mybir as mybir
from concourse.bass_utils import run_bass_kernel_spmd

F32 = mybir.dt.float32
BF16 = mybir.dt.bfloat16
I32 = mybir.dt.int32
AF = mybir.ActivationFunctionType
ALU = mybir.AluOpType
AX = mybir.AxisListType
IOA = bass.IndirectOffsetOnAxis

B, N, D = 8, 4096, 1024
H = 16
NCH = 32
STR = 1028
NSUP = 1024
NG = NSUP // 128
NITER = 49
NBIS = 46
DH, DL = 4096, 512
BIGF = float(1 << 20)
PIH = 1.5707963267948966

ENGINES = ("sync", "scalar", "vector", "tensor", "gpsimd")

DEBUG_SHAPES = {
    "mod": [1, 6 * D], "modF": [128, 48], "p12": [128, 1024],
    "x": [128, NCH * STR], "hbf": [128, 1024], "s": [128, NCH],
    "es": [128, NCH], "T": [128, 1], "mask": [128, NCH], "msup": [128, NCH],
    "slots": [128, NCH], "carry": [4, 512], "u": [128, NCH], "lohi": [1, 4],
}


class Prog:

    def __init__(self):
        self.ops = []
        self.cnt = {}

    def val(self, sem):
        return self.cnt.get(sem, 0)

    def raw(self, engine, fn, waits=(), incs=(), label="", qsem=None):
        assert engine in ENGINES, engine
        waits = [w for w in waits if w is not None]
        for w in waits:
            if isinstance(w, int):
                assert w < len(self.ops), f"wait on future op at {label}"
        dmav = None
        if qsem is not None:
            self.cnt[qsem] = self.cnt.get(qsem, 0) + 16
            dmav = self.cnt[qsem]
        for s, k in incs:
            self.cnt[s] = self.cnt.get(s, 0) + k
        self.ops.append(dict(engine=engine, fn=fn, waits=waits, label=label,
                             qsem=qsem, dmav=dmav, extra_incs=list(incs),
                             marked=False))
        return len(self.ops) - 1

    def emit(self, block, sems):
        ops = self.ops
        for i, o in enumerate(ops):
            for w in o["waits"]:
                if isinstance(w, int):
                    p = ops[w]
                    if p["qsem"] is None and not (
                            p["engine"] == o["engine"] == "tensor"):
                        p["marked"] = True
        ecnt = {}
        esem = {"sync": "sS", "scalar": "sA", "vector": "sV",
                "tensor": "sT", "gpsimd": "sG"}
        for o in ops:
            if o["marked"]:
                e = o["engine"]
                ecnt[e] = ecnt.get(e, 0) + 1
                o["val"] = ecnt[e]

        epos = {}
        for o in ops:
            e = o["engine"]
            o["pos"] = epos.get(e, 0)
            epos[e] = o["pos"] + 1

        def body(engine_name):
            def go(eng):
                last_drain = -1
                for o in ops:
                    if o["engine"] != engine_name:
                        continue
                    if o["label"] == "__drain__":
                        last_drain = o["pos"]
                        o["fn"](eng)
                        continue
                    merged = {}
                    for w in o["waits"]:
                        if isinstance(w, int):
                            p = ops[w]
                            if p["qsem"] is not None:
                                merged[p["qsem"]] = max(merged.get(p["qsem"], 0), p["dmav"])
                            elif p["engine"] == engine_name and engine_name == "tensor":
                                pass
                            elif p["marked"]:
                                s = esem[p["engine"]]
                                merged[s] = max(merged.get(s, 0), p["val"])
                        else:
                            s, v = w
                            merged[s] = max(merged.get(s, 0), v)
                    for s, v in merged.items():
                        eng.wait_ge(sems[s], v)
                    inst = o["fn"](eng)
                    if inst is not None and o["label"]:
                        try:
                            inst.annotate(o["label"])
                        except Exception:
                            pass
                    if o["qsem"] is not None:
                        assert inst is not None
                        inst = inst.then_inc(sems[o["qsem"]], 16)
                    elif o["marked"]:
                        assert inst is not None, f"marked op without instruction: {o['label']}"
                        inst = inst.then_inc(sems[esem[o["engine"]]], 1)
                    for s, k in o["extra_incs"]:
                        inst = inst.then_inc(sems[s], k)
            return go
        block.sync(body("sync"))
        block.scalar(body("scalar"))
        block.vector(body("vector"))
        block.tensor(body("tensor"))
        block.gpsimd(body("gpsimd"))


class Builder:
    def __init__(self, nc, dbg=()):
        self.nc = nc
        self.pr = Prog()
        self.T = {}
        self.dbg = set(dbg)
        self.ctx = ExitStack()
        self.M = {}

    def dram(self, name, shape, dtype=F32, kind="ExternalInput"):
        self.T[name] = self.nc.dram_tensor(name, list(shape), dtype, kind=kind).ap()
        return self.T[name]

    def sb(self, name, shape, dtype=F32):
        h = self.ctx.enter_context(self.nc.sbuf_tensor(name, list(shape), dtype))
        self.T[name] = h
        return h

    def ps(self, name, shape, dtype=F32):
        h = self.ctx.enter_context(self.nc.psum_tensor(name, list(shape), dtype))
        self.T[name] = h
        return h

    def t(self, name):
        v = self.T[name]
        return v if isinstance(v, bass.AP) else v[:]

    QCHAINS = {"sync": ["qS", "qS2", "qS3"], "scalar": ["qS", "qS2", "qS3"],
               "gpsimd": ["qG", "qG2", "qG3"]}

    def dma(self, queue, dst, src, waits=(), label="", chain=None):
        if not hasattr(self, "_lastdma"):
            self._lastdma = {}
            self._rr = {}
        if chain is None:
            ch = self.QCHAINS[queue]
            k = self._rr.get(queue, 0)
            self._rr[queue] = (k + 1) % len(ch)
            q = ch[k]
        else:
            q = chain
        prevq = self._lastdma.get(q)
        m = self.pr.raw(queue, lambda e, d=dst, s=src: e.dma_start(d, s),
                        waits=list(waits) + ([prevq] if prevq is not None else []),
                        label=label, qsem=q)
        self._lastdma[q] = m
        return m

    def op(self, engine, fn, waits=(), label=""):
        return self.pr.raw(engine, fn, waits=waits, label=label)

    def dr(self, engine):
        self.pr.raw(engine, lambda e: e.drain(), label="__drain__")


def build(dbg=()):
    nc = bass.Bass("TRN2", target_bir_lowering=False)
    b = Builder(nc, dbg)
    _declare(b)
    with b.ctx:
        _prep(b)
        _attn(b)
        _tail(b)
        sems = {}
        for s in ("qS", "qS2", "qS3", "qG", "qG2", "qG3",
                  "sA", "sV", "sT", "sG", "sS"):
            sems[s] = b.ctx.enter_context(nc.semaphore(s))
        with nc.Block() as block:
            b.pr.emit(block, sems)
    return nc


def _declare(b):
    d, sb, ps = b.dram, b.sb, b.ps
    d("x_in", [N, D]); d("cvecF", [128, 8]); d("wmod", [D, 6 * D])
    d("WV", [D, D]); d("OW", [D, D]); d("K123", [D, 48])
    d("trigF", [128, 40])
    d("rtok", [1, D]); d("lgF", [128, 8]); d("hgF", [128, 8])
    d("lw1", [D, DL]); d("lw2", [DL, D]); d("hw1", [D, DH]); d("hw2", [DH, D])
    d("lb1", [1, DL]); d("lb2", [1, D]); d("hb1", [1, DH]); d("hb2", [1, D])
    d("c_U", [128, 128]); d("c_Us", [128, 128]); d("c_id", [128, 128])
    d("c_ones1", [1, 128]); d("c_ones2", [128, 128])
    d("c_rs", [4, 3 * 128]); d("c_oh3", [128, 9])
    d("c_rrow", [128, 128]); d("c_ncrow", [128, NCH]); d("c_iotaw", [128, NCH * 8], I32)
    d("y", [N, D], kind="ExternalOutput")
    d("d_mod", [1, 6 * D], kind="Internal")
    d("d_x", [N, STR], kind="Internal")
    d("d_goff", [NSUP, 8], I32, kind="Internal")
    for nm in b.dbg:
        d("dbg_" + nm, DEBUG_SHAPES[nm], kind="ExternalOutput")

    ps("pVA", [128, 512]); ps("pVB", [128, 512]); ps("pX", [128, 512])
    ps("pCA", [128, 512]); ps("pCB", [128, 512]); ps("pCR", [128, 512])
    ps("pT", [128, 1024], BF16); ps("pTOT", [128, 512])

    sb("x", [128, NCH * STR])
    sb("WVb", [128, 8 * 1024], BF16)
    sb("OWb", [128, 8 * 1024], BF16)
    sb("K123b", [128, 8 * 48], BF16)
    sb("Ub", [128, 128], BF16)
    sb("idb", [128, 128], BF16)
    sb("ones1b", [1, 128], BF16); sb("ones1f", [1, 128], F32)
    sb("onescol", [128, 1], F32)
    sb("rsb", [4, 3 * 128], BF16); sb("oh3b", [128, 9], BF16)
    sb("biasrow", [1, 1072], BF16)

    sb("CB1b", [128, 1024]); sb("SBb", [128, 1024]); sb("Rb", [128, 1024])
    sb("carry_acc", [4, 512])
    sb("cvt", [128, 1024], BF16)
    sb("sbr_col", [128, 1])
    sb("p1s", [128, 512]); sb("p2s", [128, 512])
    sb("s_sb", [128, NCH]); sb("es", [128, NCH])
    sb("hbf", [128, 1024], BF16); sb("hT", [128, 1024], BF16)
    sb("crbf", [128, 16], BF16)
    sb("carry", [4, 512], BF16)
    sb("big1", [128, 2048])
    sb("outbf", [128, 1024], BF16); sb("outT", [128, 1024], BF16)
    sb("cp1", [128, 16]); sb("cpc", [128, 16]); sb("dxc", [128, 32]); sb("dx", [128, 32])
    sb("p2c", [128, 16]); sb("p2e", [128, 16]); sb("pe2x", [128, 16])
    sb("den", [128, 16]); sb("rden", [128, 16]); sb("dent", [128, 16])
    sb("bst", [128, 8]); sb("bcnt", [128, 1], BF16)
    sb("h2", [128, 1024])
    sb("tA", [128, 1]); sb("tB", [128, 1]); sb("tC", [128, 1]); sb("tD", [128, 1])
    sb("tE", [128, 1]); sb("tG", [128, 1]); sb("tH", [128, 1])
    sb("r0", [1, 8]); sb("r1", [1, 8]); sb("kcol", [128, 6])

    T = b.T
    x = T["x"]
    base = 17 * STR

    def xv(off, nfloat):
        assert off + nfloat <= (NCH - 17) * STR, (off, nfloat)
        return x[:, base + off: base + off + nfloat]
    T["v_g1"] = xv(0, 1024)
    T["v_g2"] = xv(1024, 1024)
    T["v_sv"] = xv(2048, 256)
    T["v_cvv"] = xv(2304, 256)
    T["v_su"] = xv(2560, 128)
    T["v_cu"] = xv(2688, 128)
    T["v_rr"] = xv(2816, 128)
    T["v_ncr"] = xv(2944, 32)
    T["v_wpcA"] = xv(3072, 1024).bitcast(BF16)
    T["v_wpcB"] = xv(4096, 1024).bitcast(BF16)
    T["v_GAb"] = xv(5120, 1024)
    T["v_row1"] = xv(6144, 1024)[0:1, :]
    T["v_row2"] = xv(7168, 1024)[0:1, :]
    T["v_stg"] = xv(8192, 128)[0:48, :]
    T["modrow"] = xv(8704, 6144)[0:1, :]
    T["modF"] = xv(14848, 48)
    T["modFb"] = xv(14896, 24).bitcast(BF16)
    T["trigp"] = xv(14920, 40)
    T["tF3"] = T["p2s"][:, 96:128]
    T["slot_i32"] = T["p2s"][:, 384:416].bitcast(I32)
    T["eqb"] = T["p2s"][:, 416:432].bitcast(BF16)
    T["supb"] = T["p2s"][:, 432:448].bitcast(BF16)
    T["rowP"] = T["p2s"][:, 448:480][0:1, :]
    T["rowQ"] = T["p2s"][:, 480:512][0:1, :]
    T["rowR"] = T["p1s"][:, 480:512][0:1, :]
    T["hb2gb"] = T["carry_acc"][:, 0:512].bitcast(BF16)[0:1, :]
    T["tF"] = T["p1s"][:, 416:448]
    T["tF2"] = T["p1s"][:, 448:480]
    T["t1v"] = T["big1"][:, 0:1024]
    T["t2v"] = T["big1"][:, 1024:2048]
    T["z2"] = T["big1"][:, 1024:2048]
    T["z2a"] = T["big1"][:, 0:1024]
    T["hb1bA"] = T["Rb"][:, 0:1024].bitcast(BF16)[0:1, :]
    T["hb1bB"] = T["big1"][:, 0:1024].bitcast(BF16)[0:1, :]
    T["iotaw"] = T["p1s"][:, 0:256].bitcast(I32)
    T["msk"] = T["p1s"][:, 256:288]
    T["gehi"] = T["p1s"][:, 288:320]
    T["eqm"] = T["p1s"][:, 320:352]
    T["u_sb"] = T["p1s"][:, 352:384]
    T["msup"] = T["p1s"][:, 384:416]
    T["goff"] = T["p2s"][:, 0:NG * 8].bitcast(I32)
    T["lb1b"] = T["p2s"][:, 128:384].bitcast(BF16)[0:1, :]
    T["cvA"] = T["cvt"][:, 0:512]
    T["cvB"] = T["cvt"][:, 512:1024]
    T["GBb2"] = T["Rb"][:]
    T["lb2gb"] = T["biasrow"][0:1, 0:1024]
    T["v_lw1b"] = T["WVb"][:, 0:4096]
    T["v_lw2b"] = T["WVb"][:, 4096:8192]
    T["v_xg2"] = T["WVb"][:, 0:2056].bitcast(F32)
    T["hT2"] = T["WVb"][:, 4096:5120]
    T["v_hw1b"] = x[:, 0:16384].bitcast(BF16)
    T["v_hw2b"] = x[:, 16448:32832].bitcast(BF16)
    T["v_xg"] = T["OWb"][:, 0:2056].bitcast(F32)
    T["v_hfA"] = T["OWb"][:, 2056:4104].bitcast(F32)
    T["v_hfB"] = T["OWb"][:, 4104:6152].bitcast(F32)
    T["v_gob"] = T["OWb"][:, 6152:7176]
    T["onesq"] = T["OWb"][:, 7176:7432].bitcast(F32)
    T["onesqb"] = T["OWb"][:, 7432:7560]


def _prep(b):
    t, op, dma, dr, M = b.t, b.op, b.dma, b.dr, b.M
    pr = b.pr
    x = t("x")

    for i, v in enumerate((PIH, 1.0 / D, 1e-5, -0.5, 1e-24, 3.4657359027997265)):
        op("vector", lambda e, i=i, v=v: e.memset(t("kcol")[:, i:i + 1], v), label=f"kc{i}")
    M["kcol"] = op("vector", lambda e: e.engine_nop(), label="kcol")
    M["idb"] = dma("gpsimd", t("idb")[:], t("c_id"), label="ld_idb")
    M["Ub"] = dma("gpsimd", t("Ub")[:], t("c_U"), label="ld_U")
    M["o1b"] = dma("gpsimd", t("ones1b")[:], t("c_ones1"), label="ld_o1b")
    M["rsb"] = dma("gpsimd", t("rsb")[:], t("c_rs"), label="ld_rs")
    M["oh3b"] = dma("gpsimd", t("oh3b")[:], t("c_oh3"), label="ld_oh3")
    M["o1f"] = dma("sync", t("ones1f")[:], t("c_ones1"), label="ld_o1f")
    M["onescol"] = dma("sync", t("onescol")[:], t("c_ones1").rearrange("o p -> p o"), label="ld_onescol")
    M["trig"] = dma("sync", t("trigp")[:], t("trigF"), label="ld_trig")
    M["rr"] = dma("sync", t("v_rr"), t("c_rrow"), label="ld_rrow")
    M["ncr"] = dma("sync", t("v_ncr"), t("c_ncrow"), label="ld_ncrow")

    m_cv = dma("sync", t("h2")[:, 0:8], t("cvecF"), label="ld_cvec")
    m_silu = op("scalar", lambda e: e.activation(t("hbf")[:, 0:8], t("h2")[:, 0:8], AF.Silu),
                waits=[m_cv], label="silu_c")

    b.mx = [None] * NCH
    for c in range(17):
        b.mx[c] = dma("sync", x[:, c * STR: c * STR + 1024],
                      t("x_in")[c * 128:(c + 1) * 128, :], label=f"ld_x{c}")
    for c in range(17):
        op("vector", lambda e, c=c: e.memset(x[:, c * STR + 1024: (c + 1) * STR], 0.0),
           waits=[b.mx[c]], label=f"padz{c}")

    qb = ["pVA", "pVB", "pX", "pCA"]
    wp = {0: t("v_wpcA"), 1: t("v_wpcB")}
    prev_use = {0: None, 1: None}
    m_quad_ev = {}
    mm = None
    for q in range(3):
        for s8 in range(8):
            buf = (q * 8 + s8) % 2
            w = [] if prev_use[buf] is None else [prev_use[buf]]
            mld = dma("gpsimd", wp[buf],
                      t("wmod")[s8 * 128:(s8 + 1) * 128, q * 2048:(q + 1) * 2048],
                      waits=w, label=f"ld_wm{q}_{s8}")
            for g in range(4):
                wts = [mld, m_silu] + ([mm] if mm is not None else [])
                if s8 == 0 and g == 0 and q > 0:
                    wts.append(m_quad_ev[q - 1])
                mm = op("tensor", lambda e, s8=s8, g=g, buf=buf: e.matmul(
                    t(qb[g])[0:1, :], t("hbf")[:, s8:s8 + 1],
                    wp[buf][:, g * 512:(g + 1) * 512],
                    start=(s8 == 0), stop=(s8 == 7)),
                    waits=wts, label=f"modmm{q}_{s8}_{g}")
            prev_use[buf] = mm
        ev = None
        for g in range(4):
            ev = op("scalar", lambda e, q=q, g=g: e.copy(
                t("modrow")[:, q * 2048 + g * 512: q * 2048 + (g + 1) * 512],
                t(qb[g])[0:1, :]), waits=[mm], label=f"modev{q}_{g}")
        m_quad_ev[q] = ev
    mq0, mq1, mq2 = m_quad_ev[0], m_quad_ev[1], m_quad_ev[2]
    M["modmm_done"] = mq2

    row1, row2 = t("v_row1"), t("v_row2")
    mr = t("modrow")
    m_ca1 = op("vector", lambda e: e.tensor_scalar(mr[:, 1024:2048], mr[:, 1024:2048],
                                                   1.0, None, ALU.add), waits=[mq0], label="ca1")
    m_gs = op("scalar", lambda e: e.sign(row1, mr[:, 2048:3072]), waits=[mq1], label="ga_s")
    m_ga = op("scalar", lambda e: e.activation(row2, mr[:, 2048:3072], AF.Abs),
              waits=[mq1, m_gs], label="ga_a")
    m_gm = op("vector", lambda e: e.tensor_scalar(row2, row2, 0.25, None, ALU.max),
              waits=[m_ga], label="ga_m")
    m_gac = op("vector", lambda e: e.tensor_mul(mr[:, 2048:3072], row1, row2),
               waits=[m_gs, m_gm], label="ga_c")
    dr("vector")
    m_cb1 = op("vector", lambda e: e.tensor_scalar(mr[:, 4096:5120], mr[:, 4096:5120],
                                                   1.0, None, ALU.add), waits=[mq2], label="cb1")
    m_gs2 = op("scalar", lambda e: e.sign(row1, mr[:, 5120:6144]),
               waits=[mq2, m_gac], label="gb_s")
    m_ga2 = op("scalar", lambda e: e.activation(row2, mr[:, 5120:6144], AF.Abs),
               waits=[mq2, m_gac, m_gs2], label="gb_a")
    m_gm2 = op("vector", lambda e: e.tensor_scalar(row2, row2, 0.25, None, ALU.max),
               waits=[m_ga2], label="gb_m")
    m_gbc = op("vector", lambda e: e.tensor_mul(mr[:, 5120:6144], row1, row2),
               waits=[m_gs2, m_gm2], label="gb_c")
    m_modrow = op("vector", lambda e: e.engine_nop(),
                  waits=[m_ca1, m_gac, m_cb1, m_gbc], label="modrow_done")
    if "mod" in b.dbg:
        dma("sync", t("dbg_mod"), mr[:], waits=[m_modrow], label="dbg_mod")

    m_dw = dma("sync", t("d_mod"), mr[:], waits=[m_modrow], label="mod2dram")
    m_drd = dma("sync", t("v_stg"), t("d_mod").rearrange("o (s p) -> (o s) p", p=128),
                waits=[m_dw], label="mod4dram")
    m_dc = op("scalar", lambda e: e.activation(t("hT")[0:48, 0:128], t("v_stg"), AF.Copy),
              waits=[m_drd], label="modF_c")
    m_dt = op("tensor", lambda e: e.transpose(t("pT")[:, 0:48], t("hT")[0:48, 0:128],
              t("idb")[0:48, 0:48]), waits=[m_dc, M["idb"], mq2], label="modF_T")
    m_modF = op("scalar", lambda e: e.copy(t("modF")[:], t("pT")[:, 0:48]),
                waits=[m_dt], label="modF_ev")
    dr("scalar")
    m_modFb = op("scalar", lambda e: e.copy(t("modFb")[:], t("modF")[:]),
                 waits=[m_modF], label="modFb")
    M["modF"] = m_modF
    if "modF" in b.dbg:
        dma("sync", t("dbg_modF"), t("modF")[:], waits=[m_modF], label="dbg_modF")

    m_rt = dma("sync", row1, t("rtok"), waits=[m_gbc], label="ld_rtok")
    last_b = mq2

    def bcast128(dstname, src_row, extraw, label):
        nonlocal last_b
        for hcol in range(2):
            mmx = op("tensor", lambda e, hcol=hcol, src_row=src_row: e.matmul(
                t("pVB")[:, 0:512], t("ones1f"),
                src_row[:, hcol * 512:(hcol + 1) * 512], start=True, stop=True),
                waits=[last_b, M["o1f"]] + extraw, label=f"{label}_mm{hcol}")
            last_b = op("scalar", lambda e, hcol=hcol, dstname=dstname: e.copy(
                t(dstname)[:, hcol * 512:(hcol + 1) * 512], t("pVB")[:, 0:512]),
                waits=[mmx], label=f"{label}_ev{hcol}")
        return last_b

    M["CB1b"] = bcast128("CB1b", mr[:, 4096:5120], [m_cb1], "CB1b")
    M["SBb"] = bcast128("SBb", mr[:, 3072:4096], [mq1], "SBb")
    M["Rb"] = bcast128("Rb", row1, [m_rt], "Rb")
    M["GAb"] = bcast128("v_GAb", mr[:, 2048:3072], [m_gac], "GAb")

    m_sm = op("vector", lambda e: e.tensor_mul(row2, mr[:, 3072:4096], row1),
              waits=[m_rt, mq1, m_gm2, M["Rb"]], label="sbr_mul")
    dr("vector")
    m_sr = op("vector", lambda e: e.reduce_sum(t("r0")[:, 0:1], row2, axis=AX.X),
              waits=[m_sm], label="sbr_red")
    m_sb1 = op("tensor", lambda e: e.matmul(t("pVB")[:, 0:1], t("ones1f"), t("r0")[:, 0:1],
                                            start=True, stop=True),
               waits=[m_sr, last_b], label="sbr_mm")
    M["sbr"] = op("scalar", lambda e: e.copy(t("sbr_col")[:], t("pVB")[:, 0:1]),
                  waits=[m_sb1], label="sbr_ev")

    M["WVb"] = dma("gpsimd", t("WVb")[:].rearrange("p (s o) -> p s o", s=8),
                   t("WV").rearrange("(s p) o -> p s o", p=128), label="ld_WV")
    M["K123b"] = dma("gpsimd", t("K123b")[:].rearrange("p (s o) -> p s o", s=8),
                     t("K123").rearrange("(s p) o -> p s o", p=128), label="ld_K123")
    M["OWb"] = dma("gpsimd", t("OWb")[:].rearrange("p (s o) -> p s o", s=8),
                   t("OW").rearrange("(s p) o -> p s o", p=128), label="ld_OW")
    mm = None
    for grp, (lo, hi, wt) in enumerate(((0, 512, "WVb"), (512, 1024, "WVb"), (0, 48, "K123b"))):
        for s8 in range(8):
            base_ = s8 * (1024 if wt == "WVb" else 48)
            wts = [m_modFb, M[wt]] + ([mm] if mm is not None else [])
            mm = op("tensor", lambda e, s8=s8, lo=lo, hi=hi, wt=wt, base_=base_: e.matmul(
                t("pCB")[0:1, 0:hi - lo], t("modFb")[:, s8:s8 + 1],
                t(wt)[:, base_ + lo: base_ + hi],
                start=(s8 == 0), stop=(s8 == 7)),
                waits=wts, label=f"bias_mm{grp}_{s8}")
        dst = ((0, 512), (512, 1024), (1024, 1072))[grp]
        mm = op("scalar", lambda e, dst=dst: e.copy(
            t("biasrow")[:, dst[0]:dst[1]], t("pCB")[0:1, 0:dst[1] - dst[0]]),
            waits=[mm], label=f"bias_ev{grp}")
    m_bias = mm

    m_wvf = m_bias
    for s in range(8):
        m_wvf = op("vector", lambda e, s=s: e.tensor_scalar(
            t("WVb")[:, s * 1024:(s + 1) * 1024], t("WVb")[:, s * 1024:(s + 1) * 1024],
            t("modF")[:, 8 + s: 9 + s], None, ALU.mult),
            waits=[M["WVb"], m_modF, m_bias, m_wvf], label=f"wvf{s}")
    for s in range(8):
        m_wvf = op("vector", lambda e, s=s: e.tensor_scalar(
            t("K123b")[:, s * 48:(s + 1) * 48], t("K123b")[:, s * 48:(s + 1) * 48],
            t("modF")[:, 8 + s: 9 + s], None, ALU.mult),
            waits=[M["K123b"], m_modF, m_bias, m_wvf], label=f"k123f{s}")
    M["wfold"] = m_wvf

    m_owf = m_wvf
    for s in range(8):
        m_owf = op("vector", lambda e, s=s: e.tensor_tensor(
            t("OWb")[:, s * 1024:(s + 1) * 1024], t("OWb")[:, s * 1024:(s + 1) * 1024],
            t("v_GAb"), ALU.mult),
            waits=[M["OWb"], M["GAb"], m_owf], label=f"owf{s}")
    M["owfold"] = m_owf
    if "wf" in b.dbg:
        dma("sync", t("dbg_wf"), t("WVb")[:], waits=[m_wvf], label="dbg_wf")


    tg = t("trigp")
    m_tb = op("vector", lambda e: e.tensor_scalar(t("dx")[:, 0:8], tg[:, 8:16],
                                                  PIH, None, ALU.add),
              waits=[M["trig"]], label="b1pi")
    m_tb2 = op("vector", lambda e: e.tensor_scalar(t("dxc")[:, 0:8], tg[:, 24:32],
                                                   PIH, None, ALU.add),
               waits=[M["trig"], m_tb], label="b2pi")
    prev_g = {}
    prev_ev = mq2
    for ser in range(2):
        acol = tg[:, 16 * ser: 16 * ser + 8]
        bcol = tg[:, 16 * ser + 8: 16 * ser + 16]
        bpi = t("dx") if ser == 0 else t("dxc")
        pdst = "p1s" if ser == 0 else "p2s"
        for k in range(8):
            w0 = [M["rr"], M["ncr"], M["trig"], m_tb, m_tb2]
            wsu = list(w0)
            if (ser, k) != (0, 0):
                wsu.append(prev_g["g1"])
            msu = op("scalar", lambda e, acol=acol, k=k: e.activation(
                t("v_su"), t("v_rr"), AF.Sin, scale=acol[:, k:k + 1]),
                waits=wsu, label=f"sinu{ser}_{k}")
            wcu = [msu]
            if (ser, k) != (0, 0):
                wcu.append(prev_g["g2"])
            mcu = op("scalar", lambda e, acol=acol, k=k: e.activation(
                t("v_cu"), t("v_rr"), AF.Sin, bias=t("kcol")[:, 0:1], scale=acol[:, k:k + 1]),
                waits=wcu + [M["kcol"]], label=f"cosu{ser}_{k}")
            wg1 = [msu]
            if ser == 1:
                wg1.append(prev_ev)
            mg1 = op("vector", lambda e, k=k: e.tensor_scalar(
                t("v_g1")[:, k * 128:(k + 1) * 128], t("v_su"), tg[:, 32 + k:33 + k],
                None, ALU.mult), waits=wg1, label=f"g1_{ser}_{k}")
            mg2 = op("vector", lambda e, k=k: e.tensor_scalar(
                t("v_g2")[:, k * 128:(k + 1) * 128], t("v_cu"), tg[:, 32 + k:33 + k],
                None, ALU.mult), waits=[mcu, mg1] + ([prev_ev] if ser == 1 else []),
                label=f"g2_{ser}_{k}")
            msv = op("scalar", lambda e, k=k, acol=acol, bcol=bcol: e.activation(
                t("v_sv")[:, k * 32:(k + 1) * 32], t("v_ncr"), AF.Sin,
                bias=bcol[:, k:k + 1], scale=acol[:, k:k + 1]),
                waits=w0 + [mcu] + ([prev_ev] if ser == 1 else []), label=f"sinv{ser}_{k}")
            mcv2 = op("scalar", lambda e, k=k, acol=acol, bpi=bpi: e.activation(
                t("v_cvv")[:, k * 32:(k + 1) * 32], t("v_ncr"), AF.Sin,
                bias=bpi[:, k:k + 1], scale=acol[:, k:k + 1]),
                waits=[msv], label=f"cosv{ser}_{k}")
            for hh in range(2):
                h = 2 * k + hh
                bs = 64 * hh
                mma = op("tensor", lambda e, k=k, bs=bs: e.matmul(
                    t("pX")[:, 0:32],
                    t("v_g1")[bs:bs + 64, k * 128:(k + 1) * 128],
                    t("v_cvv")[bs:bs + 64, k * 32:(k + 1) * 32],
                    start=True, stop=False),
                    waits=[mg2, mcv2, prev_ev, mq2], label=f"pmma{ser}_{h}")
                mmb = op("tensor", lambda e, k=k, bs=bs: e.matmul(
                    t("pX")[:, 0:32],
                    t("v_g2")[bs:bs + 64, k * 128:(k + 1) * 128],
                    t("v_sv")[bs:bs + 64, k * 32:(k + 1) * 32],
                    start=False, stop=True),
                    waits=[mma], label=f"pmmb{ser}_{h}")
                prev_ev = op("scalar", lambda e, h=h, pdst=pdst: e.copy(
                    t(pdst)[:, h * 32:(h + 1) * 32], t("pX")[:, 0:32]),
                    waits=[mmb], label=f"pev{ser}_{h}")
            prev_g = {"g1": mg1, "g2": mg2}
    M["p1"] = M["p2"] = prev_ev
    if "p12" in b.dbg:
        dma("sync", t("dbg_p12")[:, 0:512], t("p1s")[:], waits=[prev_ev], label="dbg_p1")
        dma("sync", t("dbg_p12")[:, 512:1024], t("p2s")[:], waits=[prev_ev], label="dbg_p2")

    M["arena_free"] = b.pr.raw(
        "vector", lambda e: e.engine_nop(),
        waits=[prev_ev, M["owfold"], M["sbr"], m_modFb, M["wfold"]],
        label="arena_free")
    for c in range(17, NCH):
        b.mx[c] = dma("sync", x[:, c * STR: c * STR + 1024],
                      t("x_in")[c * 128:(c + 1) * 128, :],
                      waits=[M["arena_free"]], label=f"ld_x{c}")
        op("vector", lambda e, c=c: e.memset(x[:, c * STR + 1024: (c + 1) * STR], 0.0),
           waits=[b.mx[c]], label=f"padz{c}")
    M["padz_all"] = op("vector", lambda e: e.engine_nop(), label="padz_all")
    def mkreg(e):
        b.bcreg = e.alloc_register("bc")
        return e.reg_mov(b.bcreg, NSUP - 1)
    b.pr.raw("gpsimd", mkreg, label="bcreg")
    M["carry0"] = op("vector", lambda e: e.memset(t("carry")[:], 0.0), label="carry0")
    M["tot0"] = op("vector", lambda e: e.memset(t("carry_acc")[:], 0.0), label="tot0")


def _attn(b):
    t, op, dma, dr, M = b.t, b.op, b.dma, b.dr, b.M
    x = t("x")
    S = [dict() for _ in range(NCH)]
    P0 = {}

    def pv_get(c, key):
        return S[c - 1].get(key) if c else None

    def warm(n):
        for k in range(n):
            op("tensor", lambda e: e.matmul(
                t("pTOT")[64:65, 0:128], t("idb")[:, 0:1], t("idb")[:],
                start=True, stop=True),
               waits=[M["idb"]] if k == 0 else [], label="warm")

    def stage_A(c):
        s = S[c]
        xc = x[:, c * STR: c * STR + 1024]
        s["mean"] = op("vector", lambda e, xc=xc: e.reduce_sum(t("tA"), xc, axis=AX.X),
                       waits=[b.mx[c], pv_get(c, "hbf")], label=f"mean{c}")
        s["negm"] = op("vector", lambda e: e.tensor_scalar(t("tB"), t("tA"), -1.0 / D,
                                                           None, ALU.mult),
                       waits=[s["mean"]], label=f"negm{c}")
        sqw = [s["negm"], pv_get(c, "hTe")]
        if c == 0:
            sqw.append(M["modF"])
        s["sq"] = op("scalar", lambda e, xc=xc: e.activation(
            t("hbf"), xc, AF.Square, bias=t("tB")[:, 0:1], accum_out=t("tC")[:, 0:1]),
            waits=sqw, label=f"sq{c}")
        s["ln1"] = op("scalar", lambda e: e.activation(t("tC"), t("tC"), AF.Ln,
                      scale=t("kcol")[:, 1:2], bias=t("kcol")[:, 2:3]),
                      waits=[s["sq"], M["kcol"]], label=f"lnv{c}")
        s["rstd"] = op("scalar", lambda e: e.activation(t("tC"), t("tC"), AF.Exp,
                       scale=t("kcol")[:, 3:4]), waits=[s["ln1"]], label=f"rstd{c}")
        hbw = [s["rstd"], s["negm"], pv_get(c, "hbf_free"),
               S[c - 2].get("sred") if c >= 2 else None]
        if c == 0:
            hbw.append(M["modmm_done"])
        s["hbf"] = op("vector", lambda e, xc=xc: e.tensor_scalar(
            t("hbf"), xc, t("tB")[:, 0:1], t("tC")[:, 0:1], ALU.add, ALU.mult),
            waits=hbw, label=f"hbf{c}")

    def stage_B(c):
        s = S[c]
        pT_prev = S[c - 2].get("oTe") if c >= 2 else (S[c - 1].get("hTe") if c else None)
        tw = [s["hbf"], s["sq"], pT_prev]
        if c == 0:
            tw.append(M["idb"])
        warm(16)
        m_tp = None
        for j in range(8):
            m_tp = op("tensor", lambda e, j=j: e.transpose(
                t("pT")[:, j * 128:(j + 1) * 128], t("hbf")[:, j * 128:(j + 1) * 128],
                t("idb")[:]), waits=(tw if j == 0 else [m_tp]), label=f"hTt{c}_{j}")
        s["hbf_free"] = m_tp
        s["hTe"] = op("scalar", lambda e: e.copy(t("hT"), t("pT")[:]),
                      waits=[m_tp, pv_get(c, "vals")], label=f"hTe{c}")

    def stage_C(c):
        s = S[c]
        warm(12)
        vw = [s["hTe"], M["wfold"], pv_get(c, "t1a")]
        mm = None
        for j in range(8):
            wj = vw if j == 0 else [mm]
            mm = op("tensor", lambda e, j=j: e.matmul(
                t("pCA")[:, 0:48], t("hT")[:, j * 128:(j + 1) * 128],
                t("K123b")[:, j * 48:(j + 1) * 48], start=(j == 0), stop=False),
                waits=wj, label=f"xp{c}_{j}")
        mm = op("tensor", lambda e: e.matmul(t("pCA")[:, 0:48], t("ones1b"),
                                             t("biasrow")[:, 1024:1072],
                                             start=False, stop=True),
                waits=[mm, M["o1b"]], label=f"bk{c}")
        s["xproj"] = mm
        for j in range(8):
            wj = [mm, pv_get(c, "t2b")] if j == 0 else [mm]
            mm = op("tensor", lambda e, j=j: e.matmul(
                t("pVA"), t("hT")[:, j * 128:(j + 1) * 128],
                t("WVb")[:, j * 1024: j * 1024 + 512], start=(j == 0), stop=False),
                waits=wj, label=f"vA{c}_{j}")
            mm = op("tensor", lambda e, j=j: e.matmul(
                t("pVB"), t("hT")[:, j * 128:(j + 1) * 128],
                t("WVb")[:, j * 1024 + 512: (j + 1) * 1024], start=(j == 0), stop=False),
                waits=[mm], label=f"vB{c}_{j}")
        mm = op("tensor", lambda e: e.matmul(t("pVA"), t("ones1b"), t("biasrow")[:, 0:512],
                                             start=False, stop=True), waits=[mm], label=f"bvA{c}")
        mm = op("tensor", lambda e: e.matmul(t("pVB"), t("ones1b"), t("biasrow")[:, 512:1024],
                                             start=False, stop=True), waits=[mm], label=f"bvB{c}")
        s["vals"] = mm

    def stage_D(c):
        s = S[c]
        s["cp1"] = op("vector", lambda e, c=c: e.tensor_tensor(
            t("cp1"), t("pCA")[:, 0:16], t("p1s")[:, c::NCH], ALU.add),
            waits=[s["xproj"], M["p1"], pv_get(c, "cpc")], label=f"cp1{c}")
        s["cpc"] = op("vector", lambda e: e.tensor_scalar(t("cpc"), t("cp1"), -20.0, 20.0,
                                                          ALU.max, ALU.min),
                      waits=[s["cp1"], pv_get(c, "cr")], label=f"cpc{c}")
        s["dxc"] = op("vector", lambda e: e.tensor_scalar(t("dxc"), t("pCA")[:, 16:48],
                                                          -20.0, 20.0, ALU.max, ALU.min),
                      waits=[s["cp1"], pv_get(c, "dx")], label=f"dxc{c}")
        s["cr"] = op("scalar", lambda e: e.activation(t("crbf"), t("cpc"), AF.Exp),
                     waits=[s["cpc"], pv_get(c, "tot")], label=f"cr{c}")
        s["dx"] = op("scalar", lambda e: e.activation(t("dx"), t("dxc"), AF.Exp),
                     waits=[s["dxc"], pv_get(c, "dn2")], label=f"dx{c}")
        s["p2c"] = op("vector", lambda e, c=c: e.tensor_scalar(
            t("p2c"), t("p2s")[:, c::NCH], -20.0, 20.0, ALU.max, ALU.min),
            waits=[M["p2"], pv_get(c, "p2e")], label=f"p2c{c}")
        s["p2e"] = op("scalar", lambda e: e.activation(t("p2e"), t("p2c"), AF.Exp),
                      waits=[s["p2c"], pv_get(c, "pe2x")], label=f"p2e{c}")
        s["pe2x"] = op("vector", lambda e: e.tensor_mul(t("pe2x"), t("p2e"), t("dx")[:, 16:32]),
                       waits=[s["p2e"], s["dx"], pv_get(c, "dn1")], label=f"pe2x{c}")
        cvw = [s["cr"], s["vals"], pv_get(c, "tot")]
        s["cvA"] = op("vector", lambda e: e.tensor_tensor(
            t("cvA").rearrange("p (h a) -> p h a", h=8),
            t("pVA")[:].rearrange("p (h a) -> p h a", h=8),
            t("crbf")[:, 0:8].unsqueeze(2).to_broadcast([128, 8, 64]),
            ALU.mult), waits=cvw, label=f"cvA{c}")
        s["cvB"] = op("vector", lambda e: e.tensor_tensor(
            t("cvB").rearrange("p (h a) -> p h a", h=8),
            t("pVB")[:].rearrange("p (h a) -> p h a", h=8),
            t("crbf")[:, 8:16].unsqueeze(2).to_broadcast([128, 8, 64]),
            ALU.mult), waits=[s["cvA"]], label=f"cvB{c}")
        s["t2a"] = op("vector", lambda e: e.tensor_tensor(
            t("t2v")[:, 0:512].rearrange("p (h a) -> p h a", h=8),
            t("pVA")[:].rearrange("p (h a) -> p h a", h=8),
            t("dx")[:, 0:8].unsqueeze(2).to_broadcast([128, 8, 64]),
            ALU.mult), waits=[s["dx"], s["vals"], s["cvB"], pv_get(c, "nm")],
            label=f"t2a{c}")
        s["t2b"] = op("vector", lambda e: e.tensor_tensor(
            t("t2v")[:, 512:1024].rearrange("p (h a) -> p h a", h=8),
            t("pVB")[:].rearrange("p (h a) -> p h a", h=8),
            t("dx")[:, 8:16].unsqueeze(2).to_broadcast([128, 8, 64]),
            ALU.mult), waits=[s["t2a"]], label=f"t2b{c}")

    def stage_E(c):
        s = S[c]
        cw = [s["cvA"], s["cvB"], pv_get(c, "carry") or M["carry0"], M["Ub"], M["rsb"],
              pv_get(c, "t1a"), s["cp1"], s["dxc"]]
        mcu = op("tensor", lambda e: e.matmul(t("pCA"), t("Ub"), t("cvA"),
                                              start=True, stop=False), waits=cw, label=f"cumA{c}")
        mcu = op("tensor", lambda e: e.matmul(t("pCA"), t("rsb")[:, 0:128], t("carry")[:, 0:512],
                                              start=False, stop=True), waits=[mcu], label=f"carA{c}")
        mcu = op("tensor", lambda e: e.matmul(t("pCB"), t("Ub"), t("cvB"),
                                              start=True, stop=False),
                 waits=[mcu, pv_get(c, "t1b")], label=f"cumB{c}")
        mcu = op("tensor", lambda e: e.matmul(t("pCB"), t("rsb")[:, 128:256], t("carry")[:, 0:512],
                                              start=False, stop=True), waits=[mcu], label=f"carB{c}")
        mcu = op("tensor", lambda e: e.matmul(t("pCR")[:, 0:16], t("Ub"), t("crbf"),
                                              start=True, stop=False),
                 waits=[mcu, pv_get(c, "dn1"), pv_get(c, "xm2")], label=f"cumR{c}")
        s["cum"] = op("tensor", lambda e: e.matmul(t("pCR")[:, 0:16], t("rsb")[:, 256:384],
                                                   t("carry")[:, 0:16],
                                                   start=False, stop=True),
                      waits=[mcu], label=f"carR{c}")
        mto = op("tensor", lambda e: e.matmul(t("pTOT")[0:3, :], t("oh3b")[:, 0:3],
                                              t("cvA"), start=True, stop=False),
                 waits=[s["cum"], pv_get(c, "ca"), M["oh3b"], M["tot0"]], label=f"totA{c}")
        mto = op("tensor", lambda e: e.matmul(t("pTOT")[0:3, :], t("oh3b")[:, 3:6],
                                              t("cvB"), start=False, stop=False),
                 waits=[mto], label=f"totB{c}")
        s["tot"] = op("tensor", lambda e: e.matmul(t("pTOT")[0:3, 0:16], t("oh3b")[:, 6:9],
                                                   t("crbf"), start=False, stop=True),
                      waits=[mto], label=f"totR{c}")
        s["ca"] = op("vector", lambda e: e.tensor_tensor(t("carry_acc")[0:3, :],
                                                         t("carry_acc")[0:3, :],
                                                         t("pTOT")[0:3, :], ALU.add),
                     waits=[s["tot"], M["tot0"]], label=f"cacc{c}")
        s["carry"] = op("vector", lambda e: e.tensor_copy(t("carry")[0:3, :],
                                                          t("carry_acc")[0:3, :]),
                        waits=[s["ca"]], label=f"carry{c}")

    def stage_F(c):
        s = S[c]
        s["t1a"] = op("vector", lambda e: e.tensor_tensor(
            t("t1v")[:, 0:512].rearrange("p (h a) -> p h a", h=8),
            t("pCA")[:].rearrange("p (h a) -> p h a", h=8),
            t("pe2x")[:, 0:8].unsqueeze(2).to_broadcast([128, 8, 64]),
            ALU.mult), waits=[s["cum"], s["pe2x"], pv_get(c, "obf"),
                              pv_get(c, "sred")], label=f"t1a{c}")
        s["t1b"] = op("vector", lambda e: e.tensor_tensor(
            t("t1v")[:, 512:1024].rearrange("p (h a) -> p h a", h=8),
            t("pCB")[:].rearrange("p (h a) -> p h a", h=8),
            t("pe2x")[:, 8:16].unsqueeze(2).to_broadcast([128, 8, 64]),
            ALU.mult), waits=[s["t1a"]], label=f"t1b{c}")
        s["dn1"] = op("vector", lambda e: e.tensor_tensor(t("den"), t("pCR")[:, 0:16],
                                                          t("pe2x"), ALU.mult),
                      waits=[s["cum"], s["pe2x"], pv_get(c, "obf"), s["t1b"]],
                      label=f"den1{c}")
        s["dn2"] = op("vector", lambda e: e.tensor_tensor(t("dent"), t("den"),
                                                          t("dx")[:, 0:16], ALU.add),
                      waits=[s["dn1"], pv_get(c, "rden")], label=f"den2{c}")
        s["rden"] = op("vector", lambda e: e.reciprocal(t("rden"), t("dent")),
                       waits=[s["dn2"], pv_get(c, "obf")], label=f"rden{c}")
        s["nm"] = op("gpsimd", lambda e: e.tensor_tensor(t("t1v"), t("t1v"), t("t2v"),
                                                         ALU.add),
                     waits=[s["t1b"], s["t2b"]], label=f"num{c}")
        s["obf"] = op("gpsimd", lambda e: e.tensor_tensor(
            t("outbf").rearrange("p (h a) -> p h a", h=16),
            t("t1v").rearrange("p (h a) -> p h a", h=16),
            t("rden").unsqueeze(2).to_broadcast([128, 16, 64]),
            ALU.mult), waits=[s["nm"], s["rden"], pv_get(c, "oTt")],
            label=f"outbf{c}")

    def stage_G(c):
        s = S[c]
        xc = x[:, c * STR: c * STR + 1024]
        if c + 1 < NCH:
            pT_prev = S[c + 1]["hTe"]
        else:
            pT_prev = S[c - 1]["oTe"]
        m_tp2 = None
        for j in range(8):
            wj = [s["obf"], pT_prev] if j == 0 else [m_tp2]
            m_tp2 = op("tensor", lambda e, j=j: e.transpose(
                t("pT")[:, j * 128:(j + 1) * 128], t("outbf")[:, j * 128:(j + 1) * 128],
                t("idb")[:]), waits=wj, label=f"oTt{c}_{j}")
        s["oTt"] = m_tp2
        s["oTe"] = op("scalar", lambda e: e.copy(t("outT"), t("pT")[:]),
                      waits=[m_tp2, pv_get(c, "outT")], label=f"oTe{c}")
        mo = None
        for j in range(8):
            wj = [s["oTe"], pv_get(c, "xm1"), M["owfold"]] if j == 0 else [mo]
            mo = op("tensor", lambda e, j=j: e.matmul(
                t("pX")[:, 0:512], t("outT")[:, j * 128:(j + 1) * 128],
                t("OWb")[:, j * 1024: j * 1024 + 512], start=(j == 0), stop=(j == 7)),
                waits=wj, label=f"opA{c}_{j}")
            wj2 = [mo, s["dn1"]] if j == 0 else [mo]
            mo = op("tensor", lambda e, j=j: e.matmul(
                t("pCR")[:, 0:512], t("outT")[:, j * 128:(j + 1) * 128],
                t("OWb")[:, j * 1024 + 512: (j + 1) * 1024], start=(j == 0), stop=(j == 7)),
                waits=wj2, label=f"opB{c}_{j}")
        s["outT"] = mo

    def stage_Gx(c):
        s = S[c]
        xc = x[:, c * STR: c * STR + 1024]
        s["xm1"] = op("vector", lambda e, xc=xc: e.tensor_tensor(
            xc[:, 0:512], xc[:, 0:512], t("pX")[:, 0:512], ALU.add),
            waits=[s["outT"]], label=f"xm1{c}")
        s["xm2"] = op("vector", lambda e, xc=xc: e.tensor_tensor(
            xc[:, 512:1024], xc[:, 512:1024], t("pCR")[:, 0:512], ALU.add),
            waits=[s["xm1"]], label=f"xm2{c}")

    def stage_H(c):
        s = S[c]
        xc = x[:, c * STR: c * STR + 1024]
        s["mean2"] = op("vector", lambda e, xc=xc: e.reduce_sum(t("tD"), xc, axis=AX.X),
                        waits=[s["xm2"], pv_get(c, "s")], label=f"mean2{c}")
        s["negm2"] = op("vector", lambda e: e.tensor_scalar(t("tE"), t("tD"), -1.0 / D,
                                                            None, ALU.mult),
                        waits=[s["mean2"]], label=f"negm2{c}")
        s["sq2"] = op("scalar", lambda e, xc=xc: e.activation(
            t("z2a"), xc, AF.Square, bias=t("tE")[:, 0:1], accum_out=t("tG")[:, 0:1]),
            waits=[s["negm2"], pv_get(c, "sred")], label=f"sq2{c}")
        s["ln2"] = op("scalar", lambda e: e.activation(t("tG"), t("tG"), AF.Ln,
                      scale=t("kcol")[:, 1:2], bias=t("kcol")[:, 2:3]),
                      waits=[s["ln2pre"] if False else s["sq2"], M["kcol"]], label=f"lnv2{c}")
        s["rstd2"] = op("scalar", lambda e: e.activation(t("tG"), t("tG"), AF.Exp,
                        scale=t("kcol")[:, 3:4]), waits=[s["ln2"]], label=f"rstd2{c}")
        s["st0"] = op("vector", lambda e, c=c: e.tensor_copy(
            x[:, c * STR + 1024: c * STR + 1025], t("tG")),
            waits=[s["rstd2"], M["padz_all"]], label=f"st0_{c}")
        s["st1"] = op("vector", lambda e, c=c: e.tensor_copy(
            x[:, c * STR + 1025: c * STR + 1026], t("tE")),
            waits=[s["negm2"], M["padz_all"], s["st0"]], label=f"st1_{c}")
        s["z2w"] = op("vector", lambda e, xc=xc: e.tensor_scalar(
            t("z2a"), xc, t("tE")[:, 0:1], t("tG")[:, 0:1], ALU.add, ALU.mult),
            waits=[s["rstd2"], s["negm2"], s["sq2"]], label=f"z2_{c}")
        s["h2a"] = op("vector", lambda e: e.tensor_tensor(t("h2"), t("z2a"), t("CB1b"),
                                                          ALU.mult),
                      waits=[s["z2w"], M["CB1b"], M["SBb"], pv_get(c, "smul")],
                      label=f"h2a{c}")
        s["h2b"] = op("vector", lambda e: e.tensor_tensor(t("h2"), t("h2"), t("SBb"),
                                                          ALU.add),
                      waits=[s["h2a"]], label=f"h2b{c}")
        s["ss2"] = op("scalar", lambda e: e.activation(t("z2a"), t("h2"), AF.Square,
                                                       accum_out=t("tH")[:, 0:1]),
                      waits=[s["h2b"], s["z2w"]], label=f"ss2{c}")
        s["lns"] = op("scalar", lambda e: e.activation(t("tH"), t("tH"), AF.Ln,
                      bias=t("kcol")[:, 4:5]), waits=[s["ss2"], M["kcol"]], label=f"lnss{c}")
        s["r32"] = op("scalar", lambda e: e.activation(t("tH"), t("tH"), AF.Exp,
                      scale=t("kcol")[:, 3:4], bias=t("kcol")[:, 5:6]),
                      waits=[s["lns"]], label=f"r32e{c}")
        s["st2"] = op("vector", lambda e, c=c: e.tensor_copy(
            x[:, c * STR + 1026: c * STR + 1027], t("tH")),
            waits=[s["r32"], M["padz_all"], s["st1"]], label=f"st2_{c}")
        s["spill"] = dma("sync", t("d_x")[c * 128:(c + 1) * 128, :],
                         x[:, c * STR:(c + 1) * STR],
                         waits=[s["st2"]], label=f"spill{c}")
        s["smul"] = op("vector", lambda e: e.tensor_tensor(t("z2a"), t("h2"), t("Rb"),
                                                           ALU.mult),
                       waits=[s["h2b"], M["Rb"], s["ss2"]], label=f"smul{c}")
        s["sred"] = op("vector", lambda e: e.reduce_sum(t("tD"), t("z2a"), axis=AX.X),
                       waits=[s["smul"], s["negm2"]], label=f"sred{c}")
        s["s"] = op("vector", lambda e, c=c: e.tensor_tensor(
            t("s_sb")[:, c:c + 1], t("tD"), t("sbr_col"), ALU.add),
            waits=[s["sred"], M["sbr"]], label=f"s{c}")

    for c in range(NCH):
        stage_A(c)
        if c > 0:
            stage_E(c - 1)
            stage_F(c - 1)
        stage_B(c)
        stage_C(c)
        if c > 0:
            stage_G(c - 1)
        stage_D(c)
        if c > 0:
            stage_Gx(c - 1)
            stage_H(c - 1)
    stage_E(NCH - 1)
    stage_F(NCH - 1)
    stage_G(NCH - 1)
    stage_Gx(NCH - 1)
    stage_H(NCH - 1)

    s31 = S[NCH - 1]
    M["stats_all"] = b.op("vector", lambda e: e.engine_nop(),
                          waits=[S[c]["st2"] for c in range(NCH)] +
                                [S[c]["s"] for c in range(NCH)],
                          label="stats_all")
    M["spills"] = [S[c]["spill"] for c in range(NCH)]
    M["WVb_last"] = s31["vals"]
    M["OWb_last"] = s31["outT"]
    M["dx_last"] = s31["dn2"]
    M["h2_last"] = s31["smul"]
    M["z2_last"] = s31["nm"]
    M["tD_last"] = s31["sred"]
    M["t1v_last"] = s31["sred"]
    M["pT_last"] = s31["oTe"]
    M["outbf_last"] = s31["oTt"]
    M["outT_last"] = s31["outT"]
    M["hbf_last"] = s31["sred"]
    M["hT_last"] = s31["vals"]
    M["carry_last"] = s31["carry"]
    M["cvt_last"] = s31["tot"]
    M["pTOT_last"] = s31["ca"]
    b.attn_P = {"sc_free": None}
    if "x" in b.dbg:
        b.dma("sync", t("dbg_x"), t("x"), waits=[M["stats_all"]], label="dbg_x")
    if "s" in b.dbg:
        b.dma("sync", t("dbg_s"), t("s_sb"), waits=[M["stats_all"]], label="dbg_s")


def _tail(b):
    t, op, dma, dr, M = b.t, b.op, b.dma, b.dr, b.M
    pr = b.pr
    x = t("x")
    SA = M["stats_all"]

    def pad(c, k):
        return x[:, c * STR + 1024 + k: c * STR + 1024 + k + 1]

    m_iot = dma("sync", t("iotaw"), t("c_iotaw"), waits=[SA, M["p1"]], label="ld_iotaw")
    m_gld = dma("sync", t("Rb")[0:1, 0:1024], t("d_mod")[0:1, 5120:6144],
                waits=[M["h2_last"]], label="ld_gbc")
    m_gm1 = op("tensor", lambda e: e.matmul(t("pX")[:, 0:512], t("ones1f"),
                                            t("Rb")[0:1, 0:512], start=True, stop=True),
               waits=[m_gld, M["o1f"]], label="gb_mm0")
    m_ge1 = op("scalar", lambda e: e.copy(t("GBb2")[:, 0:512], t("pX")[:, 0:512]),
               waits=[m_gm1], label="gb_ev0")
    m_gm2 = op("tensor", lambda e: e.matmul(t("pX")[:, 0:512], t("ones1f"),
                                            t("Rb")[0:1, 512:1024], start=True, stop=True),
               waits=[m_ge1], label="gb_mm1")
    m_ge2 = op("scalar", lambda e: e.copy(t("GBb2")[:, 512:1024], t("pX")[:, 0:512]),
               waits=[m_gm2], label="gb_ev1")
    m_GB = op("scalar", lambda e: e.nop(), waits=[m_ge1, m_ge2], label="GBb2")
    m_l2 = dma("sync", t("h2")[0:1, 0:1024], t("lb2"), waits=[M["h2_last"]], label="ld_lb2")
    m_l2g = op("vector", lambda e: e.tensor_tensor(t("lb2gb"), t("h2")[0:1, 0:1024],
                                                   t("GBb2")[0:1, 0:1024], ALU.mult),
               waits=[m_l2, m_GB, M["carry_last"]], label="lb2gb")
    m_h2r = dma("sync", t("h2")[0:1, 0:1024], t("hb2"), waits=[m_l2g], label="ld_hb2")
    m_h2g = op("vector", lambda e: e.tensor_tensor(t("hb2gb"), t("h2")[0:1, 0:1024],
                                                   t("GBb2")[0:1, 0:1024], ALU.mult),
               waits=[m_h2r, m_GB], label="hb2gb")
    m_lb1 = dma("gpsimd", t("lb1b"), t("lb1"), waits=[M["p2"], SA], label="ld_lb1")
    m_lgF = dma("sync", t("dx")[:, 0:8], t("lgF"), waits=[M["dx_last"]], label="ld_lgF")
    m_hgF = dma("sync", t("dx")[:, 8:16], t("hgF"), waits=[m_lgF], label="ld_hgF")
    m_lw1 = dma("gpsimd", t("v_lw1b").rearrange("p (s o) -> p s o", s=8),
                t("lw1").rearrange("(s p) o -> p s o", p=128),
                waits=[M["WVb_last"]], label="ld_lw1")
    m_lw2 = dma("gpsimd", t("v_lw2b").rearrange("p (s o) -> p s o", s=4),
                t("lw2").rearrange("(s p) o -> p s o", p=128),
                waits=[m_lw1], label="ld_lw2")
    m_lf = None
    for s in range(8):
        m_lf = op("vector", lambda e, s=s: e.tensor_scalar(
            t("v_lw1b")[:, s * 512:(s + 1) * 512], t("v_lw1b")[:, s * 512:(s + 1) * 512],
            t("dx")[:, s:s + 1], None, ALU.mult),
            waits=[m_lw1, m_lgF] + ([m_lf] if m_lf else []), label=f"lw1f{s}")
    for s in range(4):
        m_lf = op("vector", lambda e, s=s: e.tensor_tensor(
            t("v_lw2b")[:, s * 1024:(s + 1) * 1024], t("v_lw2b")[:, s * 1024:(s + 1) * 1024],
            t("GBb2"), ALU.mult),
            waits=[m_lw2, m_GB, m_lf], label=f"lw2f{s}")
    M["lwf"] = m_lf

    chain = []

    class Ser:
        def __init__(self):
            self.pend = []

        def add(self, engine, fn, waits=(), label=""):
            self.pend.append((engine, fn, list(waits), label))

        def emit(self, k):
            for _ in range(k):
                if not self.pend:
                    return
                engine, fn, waits, label = self.pend.pop(0)
                self.last = op(engine, fn, waits=waits + ([self.last] if hasattr(self, "last") else []),
                               label=label)
            return self.last

    M["onesq"] = dma("sync", t("onesq"), t("c_ones2"),
                     waits=[M["OWb_last"]], label="ld_onesq")
    M["onesqb"] = dma("gpsimd", t("onesqb"), t("c_ones2"),
                      waits=[M["OWb_last"]], label="ld_onesqb")
    m_mn = op("vector", lambda e: e.tensor_scalar(t("tF"), t("s_sb"), 64.0, None, ALU.min),
              waits=[SA, M["p1"]], label="es_min")
    m_es = op("scalar", lambda e: e.activation(t("es"), t("tF"), AF.Exp),
              waits=[m_mn], label="es_exp")
    M["es"] = m_es
    if "es" in b.dbg:
        dma("sync", t("dbg_es"), t("es"), waits=[m_es], label="dbg_es")

    def _red_bcnt(e, src):
        with b.nc.allow_low_precision(reason="integer chunk counts, exact in bf16"):
            return e.reduce_sum(t("bcnt"), src, axis=AX.X)

    ser = Ser()
    A = ser.add
    bst = t("bst")
    A("vector", lambda e: e.memset(bst[:, 5:6], 4096.0 / 1152.0),
      waits=[M["tD_last"], m_ge2], label="T0")
    for i in range(NITER):
        A("vector", lambda e: e.tensor_single_scalar(t("tF"), t("es"), bst[:, 5:6], ALU.min),
          waits=[m_es], label=f"Tmin{i}")
        A("vector", lambda e: e.reduce_sum(bst[:, 6:7], t("tF"), axis=AX.X), label=f"Tred{i}")
        A("tensor", lambda e: e.matmul(t("pX")[:, 4:5], t("onesq"), bst[:, 6:7],
                                       start=True, stop=True),
          waits=[M["onesq"]], label=f"Tsum{i}")
        A("vector", lambda e: e.tensor_scalar(bst[:, 5:6], t("pX")[:, 4:5],
                                              1.0 / 1152.0, None, ALU.mult), label=f"Tscl{i}")
    A("vector", lambda e: e.tensor_single_scalar(t("tF2"), t("es"), bst[:, 5:6], ALU.min),
      label="u")
    A("vector", lambda e: e.memset(bst[:, 0:1], 0.0), label="blo0")
    A("vector", lambda e: e.tensor_scalar(bst[:, 1:2], bst[:, 5:6], 2.0, None,
                                          ALU.mult), label="bhi0")
    for i in range(NBIS):
        A("vector", lambda e: e.tensor_tensor(bst[:, 2:3], bst[:, 0:1], bst[:, 1:2],
                                              ALU.add), label=f"ba{i}")
        A("vector", lambda e: e.tensor_scalar(bst[:, 2:3], bst[:, 2:3], 0.5, None,
                                              ALU.mult), label=f"bb{i}")
        A("vector", lambda e: e.tensor_single_scalar(t("msup"), t("tF2"), bst[:, 2:3],
                                                     ALU.is_ge), label=f"bc{i}")
        A("vector", lambda e: _red_bcnt(e, t("msup")), label=f"bd{i}")
        A("tensor", lambda e: e.matmul(t("pX")[:, 4:5], t("onesqb"), t("bcnt"),
                                       start=True, stop=True),
          waits=[M["onesqb"]], label=f"be{i}")
        A("vector", lambda e: e.tensor_scalar(bst[:, 3:4], t("pX")[:, 4:5], 1024.0, None,
                                              ALU.is_ge), label=f"bf{i}")
        A("vector", lambda e: e.tensor_tensor(bst[:, 4:5], bst[:, 2:3], bst[:, 0:1],
                                              ALU.subtract), label=f"bg{i}")
        A("vector", lambda e: e.scalar_tensor_tensor(bst[:, 0:1], bst[:, 4:5], bst[:, 3:4],
                                                     bst[:, 0:1], ALU.mult, ALU.add),
          label=f"bh{i}")
        A("vector", lambda e: e.tensor_tensor(bst[:, 4:5], bst[:, 1:2], bst[:, 2:3],
                                              ALU.subtract), label=f"bi{i}")
        A("vector", lambda e: e.scalar_tensor_tensor(bst[:, 1:2], bst[:, 4:5], bst[:, 3:4],
                                                     bst[:, 2:3], ALU.mult, ALU.add),
          label=f"bj{i}")

    def mask_build(qval, mdst, tag):
        A("vector", lambda e: e.tensor_single_scalar(t("gehi"), t("tF2"), bst[:, 1:2],
                                                     ALU.is_ge), label=f"{tag}ge")
        A("vector", lambda e: e.tensor_single_scalar(t("tF3"), t("tF2"), bst[:, 0:1],
                                                     ALU.is_ge), label=f"{tag}gl")
        A("vector", lambda e: e.tensor_tensor(t("eqm"), t("tF3"), t("gehi"), ALU.subtract),
          label=f"{tag}eq")
        A("vector", lambda e: _red_bcnt(e, t("gehi")), label=f"{tag}r")
        A("tensor", lambda e: e.matmul(t("pX")[:, 4:5], t("onesqb"), t("bcnt"),
                                       start=True, stop=True), label=f"{tag}s")
        A("vector", lambda e, qval=qval: e.tensor_scalar(bst[:, 7:8], t("pX")[:, 4:5],
                                                         -1.0, qval, ALU.mult, ALU.add),
          label=f"{tag}q")
        A("vector", lambda e: e.tensor_copy(t("eqb"), t("eqm")), label=f"{tag}eb")
        A("tensor", lambda e: e.matmul(t("pX")[0:1, 64:96], t("Ub")[:, 127:128], t("eqb"),
                                       start=True, stop=True), waits=[M["Ub"]], label=f"{tag}t")
        A("vector", lambda e: e.tensor_copy(t("rowP"), t("pX")[0:1, 64:96]), label=f"{tag}tp")
        for sft in (1, 2, 4, 8, 16):
            A("vector", lambda e, sft=sft: e.tensor_tensor(
                t("rowQ")[:, sft:NCH], t("rowP")[:, sft:NCH], t("rowP")[:, 0:NCH - sft],
                ALU.add), label=f"{tag}p{sft}")
            A("vector", lambda e, sft=sft: e.tensor_copy(t("rowQ")[:, 0:sft], t("rowP")[:, 0:sft]),
              label=f"{tag}pc{sft}")
            A("vector", lambda e: e.tensor_copy(t("rowP"), t("rowQ")), label=f"{tag}px{sft}")
        A("vector", lambda e: e.tensor_tensor(t("rowP"), t("rowP"), t("pX")[0:1, 64:96],
                                              ALU.subtract), label=f"{tag}ex")
        A("tensor", lambda e: e.matmul(t("pX")[:, 32:64], t("Ub"), t("eqb"),
                                       start=True, stop=False), label=f"{tag}u")
        A("tensor", lambda e: e.matmul(t("pX")[:, 32:64], t("ones1f"), t("rowP")[0:1, 0:NCH],
                                       start=False, stop=True), label=f"{tag}ou")
        A("vector", lambda e: e.tensor_tensor(t("tF3"), t("pX")[:, 32:64], t("eqm"),
                                              ALU.subtract), label=f"{tag}rx")
        A("vector", lambda e: e.tensor_single_scalar(t("tF3"), t("tF3"), bst[:, 7:8],
                                                     ALU.is_lt), label=f"{tag}lt")
        A("vector", lambda e: e.tensor_tensor(t("tF3"), t("tF3"), t("eqm"), ALU.mult),
          label=f"{tag}ml")
        A("vector", lambda e, mdst=mdst: e.tensor_tensor(t(mdst), t("gehi"), t("tF3"), ALU.add),
          label=f"{tag}m")

    mask_build(1024.0, "msk", "m1")
    A("vector", lambda e: e.tensor_copy(t("supb"), t("msk")), label="supb")
    A("tensor", lambda e: e.matmul(t("pX")[0:1, 64:96], t("Ub")[:, 127:128], t("supb"),
                                   start=True, stop=True), label="slT")
    A("vector", lambda e: e.tensor_copy(t("rowP"), t("pX")[0:1, 64:96]), label="slTp")
    for sft in (1, 2, 4, 8, 16):
        A("vector", lambda e, sft=sft: e.tensor_tensor(
            t("rowQ")[:, sft:NCH], t("rowP")[:, sft:NCH], t("rowP")[:, 0:NCH - sft],
            ALU.add), label=f"sp{sft}")
        A("vector", lambda e, sft=sft: e.tensor_copy(t("rowQ")[:, 0:sft], t("rowP")[:, 0:sft]),
          label=f"spc{sft}")
        A("vector", lambda e: e.tensor_copy(t("rowP"), t("rowQ")), label=f"spx{sft}")
    A("vector", lambda e: e.tensor_tensor(t("rowP"), t("rowP"), t("pX")[0:1, 64:96],
                                          ALU.subtract), label="slex")
    A("tensor", lambda e: e.matmul(t("pX")[:, 32:64], t("Ub"), t("supb"),
                                   start=True, stop=False), label="slU")
    A("tensor", lambda e: e.matmul(t("pX")[:, 32:64], t("ones1f"), t("rowP")[0:1, 0:NCH],
                                   start=False, stop=True), label="slou")
    A("vector", lambda e: e.tensor_tensor(t("tF3"), t("pX")[:, 32:64], t("msk"),
                                          ALU.subtract), label="slrx")
    A("vector", lambda e: e.scalar_tensor_tensor(t("tF"), t("msk"), -BIGF, t("tF3"),
                                                 ALU.mult, ALU.add), label="slbig")
    A("vector", lambda e: e.tensor_scalar(t("tF"), t("tF"), BIGF, None, ALU.add), label="slbig2")
    A("vector", lambda e: e.tensor_copy(t("slot_i32"), t("tF")), label="slcast")
    m_p3_done_placeholder = None

    PL = {"sc_free": None,
          "z2_prev0": M["t1v_last"], "z2_prev1": M["z2_last"],
          "hbf_free": M["hbf_last"], "pT_free": M["pT_last"],
          "hT_free": M["hT_last"], "outbf_free": M["outbf_last"], "h2_free": m_h2g,
          "outT_free": M["outT_last"], "w1_free": None, "w2_free": None,
          "gelu_free": None}
    st_y = []
    nsteps = (len(ser.pend) + 23) // 24
    m_p3 = None
    m_hw1 = []
    m_hf = None
    for c in range(NCH):
        m_p3 = ser.emit(nsteps) or m_p3
        xc = x[:, c * STR: c * STR + 1024]
        zL = t("t1v") if c % 2 == 0 else t("t2v")
        zkey = "z2_prev0" if c % 2 == 0 else "z2_prev1"
        m_sc = op("vector", lambda e, c=c: e.tensor_tensor(t("den")[:, 0:1], pad(c, 0), pad(c, 2),
                                                           ALU.mult),
                  waits=[SA, PL["sc_free"]] + ([M["dx_last"]] if c == 0 else []), label=f"LBq{c}")
        m_z = op("vector", lambda e, xc=xc, zL=zL, c=c: e.tensor_scalar(
            zL, xc, pad(c, 1), t("den")[:, 0:1], ALU.add, ALU.mult),
            waits=[m_sc, PL[zkey]], label=f"Lz{c}")
        m_h2 = op("vector", lambda e, zL=zL: e.tensor_tensor(t("h2"), zL, t("CB1b"), ALU.mult),
                  waits=[m_z, PL["h2_free"]], label=f"Lh2{c}")
        m_hs = op("vector", lambda e, c=c: e.scalar_tensor_tensor(
            t("hbf"), t("SBb"), pad(c, 2), t("h2"), ALU.mult, ALU.add),
            waits=[m_h2, PL["hbf_free"]], label=f"Lhs{c}")
        m_tp = None
        for j in range(8):
            wj = [m_hs, PL["pT_free"]] if j == 0 else [m_tp]
            m_tp = op("tensor", lambda e, j=j: e.transpose(
                t("pT")[:, j * 128:(j + 1) * 128], t("hbf")[:, j * 128:(j + 1) * 128],
                t("idb")[:]), waits=wj, label=f"LT{c}_{j}")
        m_hbf_free = m_tp
        m_hTe = op("scalar", lambda e: e.copy(t("hT"), t("pT")), waits=[m_tp, PL["hT_free"]],
                   label=f"LTe{c}")
        m_pT_free1 = m_hTe
        mw = None
        for j in range(8):
            wj = [m_hTe, M["lwf"], PL["w1_free"]] if j == 0 else [mw]
            mw = op("tensor", lambda e, j=j: e.matmul(
                t("pVA"), t("hT")[:, j * 128:(j + 1) * 128],
                t("v_lw1b")[:, j * 512:(j + 1) * 512], start=(j == 0), stop=False),
                waits=wj, label=f"Lw1_{c}_{j}")
        mw = op("tensor", lambda e: e.matmul(t("pVA"), t("ones1b"), t("lb1b"),
                                             start=False, stop=True),
                waits=[mw, m_lb1], label=f"Lb1{c}")
        m_gel = op("scalar", lambda e: e.activation(t("outbf")[:, 0:512], t("pVA"), AF.Gelu),
                   waits=[mw, PL["gelu_free"], PL["outbf_free"]], label=f"Lg{c}")
        m_w1_free = m_gel
        m_tp2 = None
        for j in range(4):
            wj = [m_gel, m_hTe] if j == 0 else [m_tp2]
            m_tp2 = op("tensor", lambda e, j=j: e.transpose(
                t("pT")[:, j * 128:(j + 1) * 128], t("outbf")[:, j * 128:(j + 1) * 128],
                t("idb")[:]), waits=wj, label=f"LT2{c}_{j}")
        m_outbf_free = m_tp2
        m_oTe = op("scalar", lambda e: e.copy(t("outT")[:, 0:512], t("pT")[:, 0:512]),
                   waits=[m_tp2, PL["outT_free"]], label=f"LTe2{c}")
        m_pT_free = m_oTe
        mw2 = None
        for j in range(4):
            wj = [m_oTe, M["lwf"], PL["w2_free"]] if j == 0 else [mw2]
            mw2 = op("tensor", lambda e, j=j: e.matmul(
                t("pCA"), t("outT")[:, j * 128:(j + 1) * 128],
                t("v_lw2b")[:, j * 1024: j * 1024 + 512], start=(j == 0), stop=False),
                waits=wj, label=f"Lw2A{c}_{j}")
            mw2 = op("tensor", lambda e, j=j: e.matmul(
                t("pCB"), t("outT")[:, j * 128:(j + 1) * 128],
                t("v_lw2b")[:, j * 1024 + 512:(j + 1) * 1024], start=(j == 0), stop=False),
                waits=[mw2], label=f"Lw2B{c}_{j}")
        mw2 = op("tensor", lambda e: e.matmul(t("pCA"), t("ones1b"), t("lb2gb")[:, 0:512],
                                              start=False, stop=True),
                 waits=[mw2, m_l2g], label=f"Lb2A{c}")
        mw2 = op("tensor", lambda e: e.matmul(t("pCB"), t("ones1b"), t("lb2gb")[:, 512:1024],
                                              start=False, stop=True),
                 waits=[mw2], label=f"Lb2B{c}")
        m_xo1 = op("vector", lambda e, xc=xc, zL=zL: e.tensor_tensor(
            zL[:, 0:512], xc[:, 0:512], t("pCA"), ALU.add),
            waits=[mw2, m_h2], label=f"Lxo1{c}")
        m_xo2 = op("vector", lambda e, xc=xc, zL=zL: e.tensor_tensor(
            zL[:, 512:1024], xc[:, 512:1024], t("pCB"), ALU.add),
            waits=[m_xo1], label=f"Lxo2{c}")
        m_w2_free = m_xo2
        m_st = dma("sync", t("y")[c * 128:(c + 1) * 128, :], zL,
                   waits=[m_xo2], label=f"sty{c}")
        st_y.append(m_st)
        PL.update({"sc_free": m_z, zkey: m_st, "hbf_free": m_hbf_free, "pT_free": m_pT_free,
                   "hT_free": mw, "outbf_free": m_outbf_free, "h2_free": m_hs,
                   "outT_free": mw2, "w1_free": m_gel, "w2_free": m_xo2,
                   "gelu_free": m_tp2})
        if c == 15:
            hw1_w = st_y[0:16] + M["spills"][0:16]
            for p in range(4):
                m_hw1.append(dma(
                    "gpsimd",
                    t("v_hw1b").rearrange("p (s o) -> p s o", s=8)[:, :, p * 1024:(p + 1) * 1024],
                    t("hw1").rearrange("(s p) o -> p s o", p=128)[:, :, p * 1024:(p + 1) * 1024],
                    waits=hw1_w, label=f"ld_hw1_{p}"))
        if 17 <= c <= 24:
            s_ = c - 17
            m_hf = op("vector", lambda e, s=s_: e.tensor_scalar(
                t("v_hw1b")[:, s * 4096:(s + 1) * 4096], t("v_hw1b")[:, s * 4096:(s + 1) * 4096],
                t("dx")[:, 8 + s: 9 + s], None, ALU.mult),
                waits=m_hw1 + [m_hgF] + ([m_hf] if m_hf else []), label=f"hw1f{s_}")
    M["hw1f"] = m_hf
    m_p3 = ser.emit(10 ** 9) or m_p3
    M["p3"] = m_p3
    if "mask" in b.dbg:
        dma("sync", t("dbg_mask"), t("msk"), waits=[m_p3], label="dbg_mask")
    if "msup" in b.dbg:
        dma("sync", t("dbg_msup"), t("msup"), waits=[m_p3], label="dbg_msup")
    if "slots" in b.dbg:
        dma("sync", t("dbg_slots"), t("tF"), waits=[m_p3], label="dbg_slots")
    if "u" in b.dbg:
        dma("sync", t("dbg_u"), t("tF2"), waits=[m_p3], label="dbg_u")

    hw2_w = st_y[16:32] + M["spills"][16:32]
    m_hw2 = []
    for p in range(4):
        m_hw2.append(dma("gpsimd",
                         t("v_hw2b")[:, p * 8192:(p + 1) * 8192].rearrange("p (s o) -> p s o", s=8),
                         t("hw2")[p * 1024:(p + 1) * 1024, :].rearrange("(s p) o -> p s o", p=128),
                         waits=hw2_w, label=f"ld_hw2_{p}"))
    hw2f_mark = []
    for s in range(32):
        m_hf = op("vector", lambda e, s=s: e.tensor_tensor(
            t("v_hw2b")[:, s * 1024:(s + 1) * 1024], t("v_hw2b")[:, s * 1024:(s + 1) * 1024],
            t("GBb2"), ALU.mult),
            waits=[m_hw2[s // 8], m_GB, m_hf], label=f"hw2f{s}")
        if s % 8 == 7:
            hw2f_mark.append(m_hf)
    M["hwf"] = m_hf

    for c in range(NCH):
        q = ("qG", "qG2", "qG3")[c % 3]
        m2 = b.pr.raw("gpsimd", lambda e, c=c: e.indirect_dma_start(
            t("d_goff"), IOA(t("slot_i32")[:, c:c + 1], 0),
            t("iotaw")[:, c * 8:(c + 1) * 8], None,
            bounds_check=b.bcreg, oob_is_err=False),
            waits=[m_iot, M["p3"], b._lastdma.get(q)], qsem=q, label=f"scg{c}")
        b._lastdma[q] = m2
    m_scg = b.pr.raw("gpsimd", lambda e: e.nop(),
                     waits=[b._lastdma.get(qq) for qq in ("qG", "qG2", "qG3")],
                     label="scg_all")
    m_goff = dma("sync", t("goff").rearrange("p (g k) -> p g k", g=NG),
                 t("d_goff").rearrange("(g p) k -> p g k", p=128),
                 waits=[m_scg], label="ld_goff")
    m_hb1a = dma("gpsimd", t("hb1bA"), t("hb1")[0:1, 0:2048],
                 waits=[m_hf, M["lwf"]], label="ld_hb1a")
    m_hb1b = dma("gpsimd", t("hb1bB"), t("hb1")[0:1, 2048:4096],
                 waits=[m_hb1a, st_y[30]], label="ld_hb1b")

    def hb1slice(p):
        if p < 2:
            return t("hb1bA")[:, p * 1024:(p + 1) * 1024]
        return t("hb1bB")[:, (p - 2) * 1024:(p - 2 + 1) * 1024]

    pTOTb = t("pTOT")[:].bitcast(BF16)
    xg_t = [t("v_xg"), t("v_xg2")]
    hT_t = [t("hT"), t("hT2")]
    gob_t = [t("v_gob"), t("outbf")]
    outT_t = [t("outT"), t("cvt")]
    w1ps = [(t("pVA")[:], t("pVB")[:]), (t("pX")[:, 0:512], t("pCR")[:, 0:512])]
    PH = {"xg_free": [None, None],
          "hT_free": [PL["hT_free"], PL["outT_free"]],
          "w1_free": [None, None],
          "gob_free": [M["OWb_last"], PL["outbf_free"]],
          "outT_free": [PL["outT_free"], M["cvt_last"]],
          "pT_free": PL["pT_free"], "pTOTb_free": M["pTOT_last"],
          "hbf_free": PL["hbf_free"],
          "z2_prev": st_y[31], "h2_free": PL["h2_free"],
          "hfA_rd": None, "hfB_rd": None, "sc_free": None}
    y_all = [("qS", pr.val("qS")), ("qS2", pr.val("qS2")), ("qS3", pr.val("qS3"))]
    for g in range(NG):
        gp = g % 2
        xg = xg_t[gp]
        hT = hT_t[gp]
        hf = "v_hfA" if g % 2 == 0 else "v_hfB"
        hf_rd = "hfA_rd" if g % 2 == 0 else "hfB_rd"
        qx = ("qG", "qG2", "qG3")[g % 3]
        xgw = [m_goff, PH["xg_free"][gp], b._lastdma.get(qx)] + M["spills"]
        if g == 0:
            xgw.append(M["OWb_last"])
        if g == 1:
            xgw.append(PL["hT_free"])
        m_xg = b.pr.raw("gpsimd", lambda e, g=g, xg=xg: e.indirect_dma_start(
            xg, None, t("d_x"), IOA(t("goff")[:, g * 8: g * 8 + 1], 0)),
            waits=xgw, qsem=qx, label=f"ldxg{g}")
        b._lastdma[qx] = m_xg
        m_sc = op("vector", lambda e, xg=xg: e.tensor_tensor(
            t("rden")[:, 0:1], xg[:, 1024:1025], xg[:, 1026:1027], ALU.mult),
            waits=[m_xg, PH["sc_free"]], label=f"HBq{g}")
        m_z = op("vector", lambda e, xg=xg: e.tensor_scalar(
            t("z2"), xg[:, 0:1024], xg[:, 1025:1026], t("rden")[:, 0:1],
            ALU.add, ALU.mult), waits=[m_sc, PH["z2_prev"]], label=f"Hz{g}")
        m_h2 = op("vector", lambda e: e.tensor_tensor(t("h2"), t("z2"), t("CB1b"), ALU.mult),
                  waits=[m_z, PH["h2_free"]], label=f"Hh2{g}")
        m_hs = op("vector", lambda e, xg=xg: e.scalar_tensor_tensor(
            t("hbf"), t("SBb"), xg[:, 1026:1027], t("h2"), ALU.mult, ALU.add),
            waits=[m_h2, PH["hbf_free"]], label=f"Hhs{g}")
        m_tp = None
        for j in range(8):
            wj = [m_hs, PH["pTOTb_free"]] if j == 0 else [m_tp]
            m_tp = op("tensor", lambda e, j=j: e.transpose(
                pTOTb[:, j * 128:(j + 1) * 128], t("hbf")[:, j * 128:(j + 1) * 128],
                t("idb")[:]), waits=wj, label=f"HT{g}_{j}")
        m_hTe = op("scalar", lambda e, hT=hT: e.copy(hT, pTOTb),
                   waits=[m_tp, PH["hT_free"][gp]], label=f"HTe{g}")
        PH["hbf_free"] = m_tp
        PH["z2_prev"] = m_h2
        PH["h2_free"] = m_hs
        PH["sc_free"] = m_z
        PH["xg_free"][gp] = m_hs
        PH["pTOTb_free"] = m_hTe
        mw2last = None
        marks = {}

        def front(p):
            pp = p % 2
            wA, wB = w1ps[pp]
            mw = None
            for j in range(8):
                wj = [m_hTe, PH["w1_free"][pp]] if j == 0 else [mw]
                if g == 0 and j == 0:
                    wj.append(M["hw1f"])
                mw = op("tensor", lambda e, j=j, p=p, wA=wA, hT=hT: e.matmul(
                    wA, hT[:, j * 128:(j + 1) * 128],
                    t("v_hw1b")[:, j * 4096 + p * 1024: j * 4096 + p * 1024 + 512],
                    start=(j == 0), stop=False),
                    waits=[w for w in wj if w], label=f"Hw1A{g}_{p}_{j}")
                mw = op("tensor", lambda e, j=j, p=p, wB=wB, hT=hT: e.matmul(
                    wB, hT[:, j * 128:(j + 1) * 128],
                    t("v_hw1b")[:, j * 4096 + p * 1024 + 512: j * 4096 + (p + 1) * 1024],
                    start=(j == 0), stop=False), waits=[mw], label=f"Hw1B{g}_{p}_{j}")
            mw = op("tensor", lambda e, p=p, wA=wA: e.matmul(
                wA, t("ones1b"), hb1slice(p)[:, 0:512], start=False, stop=True),
                waits=[mw, m_hb1b], label=f"Hb1A{g}_{p}")
            mw = op("tensor", lambda e, p=p, wB=wB: e.matmul(
                wB, t("ones1b"), hb1slice(p)[:, 512:1024], start=False, stop=True),
                waits=[mw], label=f"Hb1B{g}_{p}")
            marks[("w1", p)] = mw
            gob = gob_t[pp]
            m_gA = op("scalar", lambda e, gob=gob, wA=wA: e.activation(
                gob[:, 0:512], wA, AF.Gelu),
                waits=[mw, PH["gob_free"][pp]], label=f"HgA{g}_{p}")
            m_gB = op("scalar", lambda e, gob=gob, wB=wB: e.activation(
                gob[:, 512:1024], wB, AF.Gelu), waits=[m_gA], label=f"HgB{g}_{p}")
            marks[("gel", p)] = m_gB
            PH["w1_free"][pp] = m_gB

        def back(p, mw2last):
            pp = p % 2
            gob = gob_t[pp]
            trg = t("pT")[:] if pp == 0 else pTOTb
            oT = outT_t[pp]
            m_tp2 = None
            for j in range(8):
                if j == 0:
                    wj = [marks[("gel", p)],
                          PH["pT_free"] if pp == 0 else PH["pTOTb_free"]]
                else:
                    wj = [m_tp2]
                m_tp2 = op("tensor", lambda e, j=j, gob=gob, trg=trg: e.transpose(
                    trg[:, j * 128:(j + 1) * 128], gob[:, j * 128:(j + 1) * 128],
                    t("idb")[:]), waits=wj, label=f"HT2{g}_{p}_{j}")
            PH["gob_free"][pp] = m_tp2
            m_oTe = op("vector", lambda e, oT=oT, trg=trg: e.tensor_copy(oT, trg),
                       waits=[m_tp2, PH["outT_free"][pp]], label=f"HTe2{g}_{p}")
            if pp == 0:
                PH["pT_free"] = m_oTe
            else:
                PH["pTOTb_free"] = m_oTe
            for j in range(8):
                wj = [m_oTe]
                if p == 0 and j == 0:
                    wj.append(PH[hf_rd])
                if g == 0 and j == 0:
                    wj.append(hw2f_mark[p])
                if mw2last is not None:
                    wj.append(mw2last)
                mw2last = op("tensor", lambda e, j=j, p=p, oT=oT: e.matmul(
                    t("pCA"), oT[:, j * 128:(j + 1) * 128],
                    t("v_hw2b")[:, (8 * p + j) * 1024: (8 * p + j) * 1024 + 512],
                    start=(p == 0 and j == 0), stop=False),
                    waits=[w for w in wj if w], label=f"Hw2A{g}_{p}_{j}")
                mw2last = op("tensor", lambda e, j=j, p=p, oT=oT: e.matmul(
                    t("pCB"), oT[:, j * 128:(j + 1) * 128],
                    t("v_hw2b")[:, (8 * p + j) * 1024 + 512: (8 * p + j + 1) * 1024],
                    start=(p == 0 and j == 0), stop=False),
                    waits=[mw2last], label=f"Hw2B{g}_{p}_{j}")
            PH["outT_free"][pp] = mw2last
            return mw2last

        for p in range(4):
            front(p)
            if p > 0:
                mw2last = back(p - 1, mw2last)
        mw2last = back(3, mw2last)
        PH["hT_free"][gp] = marks[("w1", 3)]
        mw2last = op("tensor", lambda e: e.matmul(t("pCA"), t("ones1b"), t("hb2gb")[:, 0:512],
                                                  start=False, stop=True),
                     waits=[mw2last, m_h2g], label=f"Hb2A{g}")
        mw2last = op("tensor", lambda e: e.matmul(t("pCB"), t("ones1b"), t("hb2gb")[:, 512:1024],
                                                  start=False, stop=True),
                     waits=[mw2last], label=f"Hb2B{g}")
        m_ev1 = op("vector", lambda e, hf=hf: e.tensor_copy(t(hf)[:, 0:512], t("pCA")),
                   waits=[mw2last, PH[hf_rd]], label=f"Hev1{g}")
        m_ev2 = op("vector", lambda e, hf=hf: e.tensor_copy(t(hf)[:, 512:1024], t("pCB")),
                   waits=[m_ev1], label=f"Hev2{g}")
        qp = ("qG", "qG2", "qG3")[g % 3]
        m_p6 = b.pr.raw("gpsimd", lambda e, hf=hf, g=g: e.indirect_dma_start(
            t("y"), IOA(t("goff")[:, g * 8: g * 8 + 1], 0), t(hf), None,
            compute_op=ALU.add),
            waits=[m_ev2, m_goff, b._lastdma.get(qp)] + y_all, qsem=qp,
            label=f"p6_{g}")
        b._lastdma[qp] = m_p6
        PH[hf_rd] = m_p6

    b.pr.raw("sync", lambda e: e.nop(),
             waits=[(q, b.pr.val(q)) for q in
                    ("qS", "qS2", "qS3", "qG", "qG2", "qG3")],
             incs=[], label="final_wait")


def _host_inputs(inputs, core):
    f = np.ascontiguousarray
    vw = np.asarray(inputs["L0_attn_value_weight"], np.float32)
    ow = np.asarray(inputs["L0_attn_output_weight"], np.float32)
    k1 = np.asarray(inputs["L0_attn_k1"], np.float32)
    k2 = np.asarray(inputs["L0_attn_k2"], np.float32)
    k3 = np.asarray(inputs["L0_attn_k3"], np.float32)
    WV = vw.transpose(1, 0, 2).reshape(D, H * 64)
    OW = ow.transpose(0, 2, 1).reshape(H * 64, D)
    K123 = np.concatenate([k1.T, k2.T, k3.T], axis=1)

    def fmaj(v):
        return f(np.asarray(v, np.float32).reshape(8, 128).T)

    def hpf(v):
        v = np.asarray(v, np.float32).reshape(H, 64)
        out = np.empty((128, 8), np.float32)
        for k in range(8):
            out[0:64, k] = v[2 * k]
            out[64:128, k] = v[2 * k + 1]
        return out

    trigF = np.concatenate([hpf(inputs["L0_attn_a1"]), hpf(inputs["L0_attn_b1"]),
                            hpf(inputs["L0_attn_a2"]), hpf(inputs["L0_attn_b2"]),
                            hpf(inputs["L0_attn_c"])], axis=1)
    r = (np.arange(128, dtype=np.float64) / (N - 1)).astype(np.float32)
    cg = ((128.0 * np.arange(NCH, dtype=np.float64)) / (N - 1)).astype(np.float32)
    rs = np.zeros((4, 3 * 128), np.float32)
    for j in range(3):
        rs[j, j * 128:(j + 1) * 128] = 1.0
    oh3 = np.zeros((128, 9), np.float32)
    for j in range(3):
        oh3[:, j * 3 + j] = 1.0
    tok = (np.arange(NCH)[None, :] * 128 + np.arange(128)[:, None]).astype(np.int32)
    return {
        "x_in": f(np.asarray(inputs["L0_attn_x"], np.float32)[core]),
        "cvecF": fmaj(np.asarray(inputs["L0_c"], np.float32)[core]),
        "wmod": f(np.asarray(inputs["L0_wmod"], np.float32)),
        "WV": f(WV), "OW": f(OW), "K123": f(K123),
        "trigF": f(trigF),
        "rtok": f(np.asarray(inputs["L0_ffn_routing_token"], np.float32)[None, :]),
        "lgF": fmaj(inputs["L0_ffn_light_gamma"]),
        "hgF": fmaj(inputs["L0_ffn_heavy_gamma"]),
        "lw1": f(np.asarray(inputs["L0_ffn_light_w1"], np.float32)),
        "lw2": f(np.asarray(inputs["L0_ffn_light_w2"], np.float32)),
        "hw1": f(np.asarray(inputs["L0_ffn_heavy_w1"], np.float32)),
        "hw2": f(np.asarray(inputs["L0_ffn_heavy_w2"], np.float32)),
        "lb1": f(np.asarray(inputs["L0_ffn_light_b1"], np.float32)[None, :]),
        "lb2": f(np.asarray(inputs["L0_ffn_light_b2"], np.float32)[None, :]),
        "hb1": f(np.asarray(inputs["L0_ffn_heavy_b1"], np.float32)[None, :]),
        "hb2": f(np.asarray(inputs["L0_ffn_heavy_b2"], np.float32)[None, :]),
        "c_U": f(np.triu(np.ones((128, 128), np.float32))),
        "c_Us": f(np.triu(np.ones((128, 128), np.float32), 1)),
        "c_id": f(np.eye(128, dtype=np.float32)),
        "c_ones1": f(np.ones((1, 128), np.float32)),
        "c_ones2": f(np.ones((128, 128), np.float32)),
        "c_rs": f(rs),
        "c_oh3": f(oh3),
        "c_rrow": f(np.broadcast_to(r, (128, 128)).copy()),
        "c_ncrow": f(np.broadcast_to(cg, (128, NCH)).copy()),
        "c_iotaw": f(np.repeat(tok[:, :, None], 8, axis=2).reshape(128, NCH * 8)),
    }


def kernel(**inputs):
    nc = build()
    in_maps = [_host_inputs(inputs, core) for core in range(B)]
    res = run_bass_kernel_spmd(nc, in_maps, core_ids=list(range(B)))
    out = np.stack([np.asarray(res.results[i]["y"]) for i in range(B)], axis=0)
    return out.astype(np.float32)
